# version 05 (final)
# baseline (speedup 1.0000x reference)
import numpy as np
import ml_dtypes
from contextlib import ExitStack

import concourse.bass as bass
import concourse.mybir as mybir
from concourse.bass_utils import run_bass_kernel_spmd

F32 = mybir.dt.float32
BF16 = mybir.dt.bfloat16
I8 = mybir.dt.int8
AF = mybir.ActivationFunctionType
ALU = mybir.AluOpType

NT = 2048
DP = 16
CH = 64
NCH = NT // CH
NB = 4
BL = 512
NG = 8
EPS = 1e-5
XNW = NT + 1
EW = BL + 1

DEBUG = False
STOP_AFTER = 6
BF = ml_dtypes.bfloat16


class Sems:
    ALL = ("TE", "VE", "SE", "GE", "CCS")

    def __init__(self, nc, es):
        self.nc, self.es, self.m = nc, es, {}

    def _e(self, name):
        if name not in self.m:
            self.m[name] = [self.es.enter_context(self.nc.semaphore(name)), 0]
        return self.m[name]

    def h(self, name):
        return self._e(name)[0]

    def inc(self, name, instr, k=1):
        e = self._e(name)
        instr.then_inc(e[0], k)
        e[1] += k
        return e[1]

    def n(self, name):
        return self._e(name)[1]

    def w(self, eng, name, val):
        if val and val > 0:
            eng.wait_ge(self.h(name), val)

    def dinc(self, slot, instr, k=16):
        return (slot, self.inc(slot, instr, k))

    def wt(self, eng, mark):
        if mark:
            self.w(eng, mark[0], mark[1])

    def fence(self, eng, name, instr):
        self.w(eng, name, self.inc(name, instr, 1))

    def snap(self):
        return {k: e[1] for k, e in self.m.items() if e[1] > 0}

    def wait_snap(self, eng, sn):
        for k, v in sn.items():
            if v > 0:
                eng.wait_ge(self.h(k), v)


class _StopEmit(Exception):
    pass


def build_program():
    nc = bass.Bass("TRN2", target_bir_lowering=False, debug=False,
                   detect_race_conditions=False)

    def din(name, shape, dt):
        return nc.dram_tensor(name, shape, dt, kind="ExternalInput").ap()

    x_h = din("x", [2048, NT], F32)
    xprev_h = din("xprev", [128, DP], F32)
    gate_h = din("gate", [128, 1], F32)
    wx1_h = din("wx1", [2048, 160], F32)
    wx2_h = din("wx2r", [32, 5 * 2048], F32)
    wr_h = din("wr", [2048, 1024], F32)
    ww1_h = din("ww1", [2048, 64], F32)
    ww2_h = din("ww2", [64, 1024], F32)
    wk_h = din("wk", [2048, 1024], F32)
    wv_h = din("wv", [2048, 2048], F32)
    wg_h = din("wg", [2048, 2048], F32)
    wo_h = din("wo", [2048, 2048], F32)
    bw_h = din("bwT_in", [128, 8], F32)
    mu_h = din("muT_in", [128, DP], F32)
    bonus_h = din("bonusT_in", [128, 8], F32)
    gnw_h = din("gnwT_in", [128, 1], F32)
    gnb_h = din("gnbT_in", [128, 1], F32)
    sg_h = din("sgT_in", [128, 64], F32)
    su_h = din("suT_in", [128, 64], F32)
    sd_h = din("sdT_in", [128, DP], F32)
    wgate8_h = din("wgate8", [64, 128, 16 * 128], I8)
    wup8_h = din("wup8", [64, 128, 16 * 128], I8)
    wdn8_h = din("wdn8", [16, 128, 64 * 128], I8)
    id128_h = din("id128_in", [128, 128], BF16)
    i64_h = din("i64_in", [64, 64], BF16)
    bm_h = din("bm_in", [128, 128], BF16)
    pm0_h = din("pm0_in", [128, 128], BF16)
    pm1_h = din("pm1_in", [128, 128], BF16)
    seg4_h = din("seg4_in", [128, 4], BF16)
    indb_h = din("indb_in", [128, 512], BF16)
    indf_h = din("indf_in", [128, 512], F32)
    bcp_h = din("bcp_in", [32, 2048], F32)
    ones1_h = din("ones1_in", [128, 1], F32)
    onesk1_h = din("onesk1_in", [1, 128], F32)
    ones64_h = din("ones64_in", [128, 64], F32)

    out_h = nc.dram_tensor("out", [2048, NT], F32, kind="ExternalOutput").ap()

    def dsc(name, shape, dt, dbg=False):
        kind = "ExternalOutput" if (DEBUG and dbg) else "Internal"
        return nc.dram_tensor(name, shape, dt, kind=kind).ap()

    wrb_s = dsc("wrb", [2048, 1024], BF16)
    wkb_s = dsc("wkb", [2048, 1024], BF16)
    wvb_s = dsc("wvb", [2048, 2048], BF16)
    wgb_s = dsc("wgb", [2048, 2048], BF16)
    wob_s = dsc("wob", [2048, 2048], BF16)
    wx2b_s = dsc("wx2b", [32, 5 * 2048], BF16)
    ffn_s = dsc("ffn", [2048, NT], F32, dbg=True)
    vtok_s = dsc("vtok", [NCH, 64, 2048], BF16, dbg=True)
    rt_s = dsc("rt", [NG, 128, NT], BF16, dbg=True)
    kt_s = dsc("kt", [NG, 128, NT], BF16, dbg=True)
    srec_s = dsc("srec", [NG, 128, NCH * 64], BF16, dbg=True)
    xlg_s = dsc("xlg", [DP, 128, NT], BF16, dbg=True)
    cc_in = dsc("cc_in", [128, 512], F32)
    cc_out = dsc("cc_out", [256, 512], F32)
    if DEBUG:
        dxn_h = nc.dram_tensor("d_xn", [128, DP * XNW], BF16,
                               kind="ExternalOutput").ap()
        de_h = nc.dram_tensor("d_e", [NB, 128, NG * EW], F32,
                              kind="ExternalOutput").ap()
        dr_h = nc.dram_tensor("d_r", [NB, 128, NG * BL], BF16,
                              kind="ExternalOutput").ap()
        do_h = nc.dram_tensor("d_o", [128, DP * NT], BF16,
                              kind="ExternalOutput").ap()
        du_h = nc.dram_tensor("d_u", [64, NG * NCH * 4], F32,
                              kind="ExternalOutput").ap()
        ds0_h = nc.dram_tensor("d_s0", [128, 512], F32,
                               kind="ExternalOutput").ap()
        dcoef_h = nc.dram_tensor("d_coef", [128, NCH * NG], F32,
                                 kind="ExternalOutput").ap()
        dem_h = nc.dram_tensor("d_em", [128, NCH * NG * 3], F32,
                               kind="ExternalOutput").ap()
        dp_h = nc.dram_tensor("d_p", [128, 320], BF16,
                              kind="ExternalOutput").ap()
        dxl_h = nc.dram_tensor("d_xl", [128, 512], BF16,
                               kind="ExternalOutput").ap()

    es = ExitStack()
    try:
      with es:
        def SB(n, s, d):
            return es.enter_context(nc.sbuf_tensor(n, s, d))

        S = Sems(nc, es)
        blk = es.enter_context(nc.Block())
        cm = {}

        id128 = SB("id128", [128, 128], BF16)
        seg4 = SB("seg4", [128, 4], BF16)
        ones1 = SB("ones1", [128, 1], F32)
        onesk1 = SB("onesk1", [1, 128], F32)
        ones64 = SB("ones64", [128, 64], F32)
        muT = SB("muT", [128, DP], F32)
        bwT = SB("bwT", [128, 8], F32)
        bonusT = SB("bonusT", [128, 8], F32)
        gnwT = SB("gnwT", [128, 1], F32)
        gnbT = SB("gnbT", [128, 1], F32)
        sgT = SB("sgT", [128, 64], F32)
        suT = SB("suT", [128, 64], F32)
        sdT = SB("sdT", [128, DP], F32)
        gateT = SB("gateT", [128, 1], F32)
        xprevT = SB("xprevT", [128, DP], F32)
        eps_sb = SB("eps_sb", [128, 1], F32)
        slive = SB("slive", [128, NG * 64], F32)
        grun = SB("grun", [128, NG], F32)
        coef = SB("coef", [128, NCH * NG], F32)
        em0s = SB("em0s", [128, NCH * NG], F32)
        em1s = SB("em1s", [128, NCH * NG], F32)
        fLrs = SB("fLrs", [128, NCH * NG], F32)
        u_all = SB("u_all", [64, NG * NCH * 4], F32)

        @blk.sync
        def _(q):
            for t, h in [(id128, id128_h), (seg4, seg4_h),
                         (ones1, ones1_h), (onesk1, onesk1_h),
                         (ones64, ones64_h), (muT, mu_h), (bwT, bw_h),
                         (bonusT, bonus_h), (gnwT, gnw_h), (gnbT, gnb_h),
                         (sgT, sg_h), (suT, su_h), (sdT, sd_h),
                         (gateT, gate_h), (xprevT, xprev_h)]:
                S.inc("Dco", q.dma_start(out=t[:, :], in_=h[:, :]), 16)
        c_consts = ("Dco", S.n("Dco"))

        @blk.gpsimd
        def _(g):
            g.memset(eps_sb[:, :], EPS)
            g.memset(slive[:, :], 0.0)
            ii = g.memset(grun[:, :], 0.0)
            cm["ginit"] = S.inc("GE", ii, 1)

        with ExitStack() as xes:
            def XSB(n, s, d):
                return xes.enter_context(nc.sbuf_tensor(n, s, d))

            xn = XSB("xn", [128, DP * XNW], BF16)
            wx1_sb = XSB("wx1_sb", [128, DP * 160], BF16)
            ww1_sb = XSB("ww1_sb", [128, DP * 64], BF16)
            ww2_sb = XSB("ww2_sb", [64, 1024], BF16)

            @blk.gpsimd
            def _(g):
                for d in range(DP):
                    S.inc("Dsw", g.dma_start(
                        out=wx1_sb[:, d * 160:(d + 1) * 160],
                        in_=wx1_h[d * 128:(d + 1) * 128, :]), 16)
                    S.inc("Dsw", g.dma_start(
                        out=ww1_sb[:, d * 64:(d + 1) * 64],
                        in_=ww1_h[d * 128:(d + 1) * 128, :]), 16)
                S.inc("Dsw", g.dma_start(out=ww2_sb[:, :],
                                         in_=ww2_h[:, :]), 16)
            cm["smallw"] = ("Dsw", S.n("Dsw"))

            with ExitStack() as pes:
                pcA = pes.enter_context(nc.sbuf_tensor("pcA", [128, 1024], BF16))
                pcB = pes.enter_context(nc.sbuf_tensor("pcB", [128, 1024], BF16))

                pc_jobs = []
                for cp in range(10):
                    pc_jobs.append((wx2_h, wx2b_s, 0, 32,
                                    cp * 1024, 1024))
                for (srch, dsth, w) in [(wr_h, wrb_s, 1024), (wk_h, wkb_s, 1024),
                                        (wv_h, wvb_s, 2048), (wg_h, wgb_s, 2048),
                                        (wo_h, wob_s, 2048)]:
                    for d in range(DP):
                        for cp in range(w // 1024):
                            pc_jobs.append((srch, dsth, d * 128, 128,
                                            cp * 1024, 1024))
                pc_ld = []

                @blk.gpsimd
                def _(g):
                    for i, (srch, dsth, r0, nr, c0, ncs) in enumerate(pc_jobs):
                        buf = pcA if i % 2 == 0 else pcB
                        if i >= 2:
                            g.wait_ge(S.h("Dps%d" % (i % 2)),
                                      16 * ((i - 2) // 2 + 1))
                        pc_ld.append(S.dinc("Dpl%d" % (i % 2), g.dma_start(
                            out=buf[0:nr, 0:ncs],
                            in_=srch[r0:r0 + nr, c0:c0 + ncs])))

                @blk.scalar
                def _(sc):
                    for i, (srch, dsth, r0, nr, c0, ncs) in enumerate(pc_jobs):
                        buf = pcA if i % 2 == 0 else pcB
                        S.wt(sc, pc_ld[i])
                        S.inc("Dps%d" % (i % 2), sc.dma_start(
                            out=dsth[r0:r0 + nr, c0:c0 + ncs],
                            in_=buf[0:nr, 0:ncs]), 16)
                cm["precast"] = [("Dps0", S.n("Dps0")), ("Dps1", S.n("Dps1"))]

                if STOP_AFTER < 1:
                    raise _StopEmit()
                _sc_A = nc.enter_named_scope("phA", False)[0]
                with ExitStack() as aes:
                    xt = [aes.enter_context(nc.sbuf_tensor(
                        "xt%d" % i, [128, 512], F32)) for i in range(4)]
                    sqb = [aes.enter_context(nc.sbuf_tensor(
                        "sqb%d" % i, [128, 512], F32)) for i in range(2)]
                    lnst = aes.enter_context(
                        nc.sbuf_tensor("lnst", [1, 4 * 512], F32))
                    rstd1 = aes.enter_context(
                        nc.sbuf_tensor("rstd1", [1, 512], F32))
                    xtm2 = [aes.enter_context(nc.sbuf_tensor(
                        "xtm%d" % i, [128, 512], F32)) for i in range(2)]
                    pvt = aes.enter_context(
                        nc.sbuf_tensor("pvt", [128, 2 * DP], F32))
                    ps_sum = aes.enter_context(
                        nc.psum_tensor("ps_sum", [1, 512], F32))
                    ps_sq = aes.enter_context(
                        nc.psum_tensor("ps_sq", [1, 512], F32))
                    ps_nm = aes.enter_context(
                        nc.psum_tensor("ps_nm", [128, 512], F32))
                    ps_rs = aes.enter_context(
                        nc.psum_tensor("ps_rs", [128, 512], F32))
                    ps_pv = aes.enter_context(
                        nc.psum_tensor("ps_pv", [1, 2 * DP], F32))
                    ps_pb = aes.enter_context(
                        nc.psum_tensor("ps_pb", [128, 2 * DP], F32))

                    @blk.scalar
                    def _(sc):
                        S.wt(sc, c_consts)
                        cm["pvsq"] = S.inc("SE", sc.activation(
                            pvt[:, 0:DP], xprevT[:, :], AF.Square), 1)

                    @blk.tensor
                    def _(te):
                        S.wt(te, c_consts)
                        for d in range(DP):
                            te.matmul(ps_pv[:, 0:1], ones1[:, :],
                                      xprevT[:, d:d + 1],
                                      start=(d == 0), stop=(d == DP - 1))
                        S.w(te, "SE", cm["pvsq"])
                        ii = None
                        for d in range(DP):
                            ii = te.matmul(ps_pv[:, 1:2], ones1[:, :],
                                           pvt[:, d:d + 1],
                                           start=(d == 0), stop=(d == DP - 1))
                        cm["pv2"] = S.inc("TE", ii, 1)

                    @blk.vector
                    def _(v):
                        S.w(v, "TE", cm["pv2"])
                        v.tensor_scalar_mul(lnst[:, 0:1], ps_pv[:, 0:1],
                                            -1.0 / 2048.0)
                        S.fence(v, "VE", v.tensor_scalar_mul(
                            lnst[:, 512:513], ps_pv[:, 1:2], 1.0 / 2048.0))
                        S.fence(v, "VE", v.tensor_mul(
                            lnst[:, 1024:1025], lnst[:, 0:1], lnst[:, 0:1]))
                        ii = v.tensor_sub(lnst[:, 1536:1537],
                                          lnst[:, 512:513],
                                          lnst[:, 1024:1025])
                        cm["pv3"] = S.inc("VE", ii, 1)

                    @blk.scalar
                    def _(sc):
                        S.w(sc, "VE", cm["pv3"])
                        S.w(sc, "GE", cm["ginit"])
                        sc.activation(rstd1[:, 0:1], lnst[:, 1536:1537],
                                      AF.Ln, bias=eps_sb[0:1, :])
                        cm["pv4"] = S.inc("SE", sc.activation(
                            rstd1[:, 0:1], rstd1[:, 0:1], AF.Exp,
                            scale=-0.5), 1)

                    @blk.tensor
                    def _(te):
                        S.w(te, "SE", cm["pv4"])
                        te.matmul(ps_pb[:, 0:1], onesk1[:, :],
                                  lnst[:, 0:1], start=True, stop=True)
                        ii = te.matmul(ps_pb[:, 1:2], onesk1[:, :],
                                       rstd1[:, 0:1], start=True, stop=True)
                        cm["pv5"] = S.inc("TE", ii, 1)

                    @blk.vector
                    def _(v):
                        S.w(v, "TE", cm["pv5"])
                        S.w(v, "TE", cm["pv2"])
                        S.fence(v, "VE", v.tensor_copy(pvt[:, 0:2],
                                                       ps_pb[:, 0:2]))
                        ii = v.tensor_scalar(pvt[:, DP:2 * DP], xprevT[:, :],
                                             pvt[:, 0:1], pvt[:, 1:2],
                                             ALU.add, ALU.mult)
                        cm["pv6"] = S.inc("VE", ii, 1)

                    @blk.vector
                    def _(v):
                        S.w(v, "VE", cm["pv6"])
                        last = None
                        for d in range(DP):
                            last = v.tensor_copy(xn[:, d * XNW:d * XNW + 1],
                                                 pvt[:, DP + d:DP + d + 1])
                        cm["pv7"] = S.inc("VE", last, 1)

                    ldmark = {}
                    consume = {}
                    li = 0
                    for b in range(NB):
                        for d in range(DP):
                            slot = li % 4

                            @blk.sync
                            def _(q, b=b, d=d, slot=slot, li=li):
                                if li - 4 in consume:
                                    sem, val = consume[li - 4]
                                    S.w(q, sem, val)
                                ldmark[li] = S.dinc(
                                    "Dxt%d" % slot, q.dma_start(
                                        out=xt[slot][:, :],
                                        in_=x_h[d * 128:(d + 1) * 128,
                                                b * BL:(b + 1) * BL]))

                            sqi = b * DP + d

                            @blk.scalar
                            def _(sc, b=b, d=d, slot=slot, li=li, sqi=sqi):
                                S.wt(sc, ldmark[li])
                                if sqi >= 2:
                                    S.w(sc, "TE", cm.get(("asq", sqi - 2)))
                                cm[("ase", sqi)] = S.inc("SE", sc.activation(
                                    sqb[sqi % 2][:, :], xt[slot][:, :],
                                    AF.Square), 1)

                            @blk.tensor
                            def _(te, b=b, d=d, slot=slot, li=li, sqi=sqi):
                                S.wt(te, ldmark[li])
                                if d == 0:
                                    S.w(te, "VE", cm.get(("axn", b - 1)))
                                te.matmul(ps_sum[:, :], ones1[:, :],
                                          xt[slot][:, :],
                                          start=(d == 0), stop=(d == DP - 1))
                                S.w(te, "SE", cm[("ase", sqi)])
                                ii = te.matmul(ps_sq[:, :], ones1[:, :],
                                               sqb[sqi % 2][:, :],
                                               start=(d == 0),
                                               stop=(d == DP - 1))
                                cm[("asq", sqi)] = S.inc("TE", ii, 1)
                                consume[li] = ("TE", cm[("asq", sqi)])
                            li += 1

                        @blk.vector
                        def _(v, b=b):
                            S.w(v, "TE", cm[("asq", b * DP + DP - 1)])
                            v.tensor_scalar_mul(lnst[:, 0:512], ps_sum[:, :],
                                                -1.0 / 2048.0)
                            S.fence(v, "VE", v.tensor_scalar_mul(
                                lnst[:, 512:1024],
                                ps_sq[:, :], 1.0 / 2048.0))
                            S.fence(v, "VE", v.tensor_mul(
                                lnst[:, 1024:1536], lnst[:, 0:512],
                                lnst[:, 0:512]))
                            ii = v.tensor_sub(lnst[:, 1536:2048],
                                              lnst[:, 512:1024],
                                              lnst[:, 1024:1536])
                            cm[("ast", b)] = S.inc("VE", ii, 1)

                        @blk.scalar
                        def _(sc, b=b):
                            S.w(sc, "VE", cm[("ast", b)])
                            sc.activation(rstd1[:, :], lnst[:, 1536:2048],
                                          AF.Ln, bias=eps_sb[0:1, :])
                            cm[("ars", b)] = S.inc("SE", sc.activation(
                                rstd1[:, :], rstd1[:, :], AF.Exp,
                                scale=-0.5), 1)

                        @blk.tensor
                        def _(te, b=b):
                            S.w(te, "VE", cm[("ast", b)])
                            if b > 0:
                                S.w(te, "VE", cm.get(("axn", b - 1)))
                            te.matmul(ps_nm[:, :], onesk1[:, :],
                                      lnst[:, 0:512], start=True, stop=True)
                            S.w(te, "SE", cm[("ars", b)])
                            ii = te.matmul(ps_rs[:, :], onesk1[:, :],
                                           rstd1[:, :], start=True, stop=True)
                            cm[("abc", b)] = S.inc("TE", ii, 1)

                        for d in range(DP):
                            slot = li % 4

                            @blk.sync
                            def _(q, b=b, d=d, slot=slot, li=li):
                                if li - 4 in consume:
                                    sem, val = consume[li - 4]
                                    S.w(q, sem, val)
                                ldmark[li] = S.dinc(
                                    "Dxt%d" % slot, q.dma_start(
                                        out=xt[slot][:, :],
                                        in_=x_h[d * 128:(d + 1) * 128,
                                                b * BL:(b + 1) * BL]))

                            @blk.vector
                            def _(v, b=b, d=d, slot=slot, li=li):
                                S.wt(v, ldmark[li])
                                S.w(v, "TE", cm[("abc", b)])
                                v.tensor_add(xtm2[li % 2][:, :],
                                             xt[slot][:, :], ps_nm[:, :])
                                dst = xn[:, d * XNW + 1 + b * BL:
                                         d * XNW + 1 + (b + 1) * BL]
                                ii = v.tensor_mul(dst, xtm2[li % 2][:, :],
                                                  ps_rs[:, :])
                                cm[("axnt", li)] = S.inc("VE", ii, 1)
                                consume[li] = ("VE", cm[("axnt", li)])
                            li += 1

                        @blk.vector
                        def _(v, b=b):
                            cm[("axn", b)] = cm[("axnt", li - 1)]
                cm["Adone"] = None
                snapA = S.snap()
                if DEBUG:
                    @blk.scalar
                    def _(sc):
                        S.wait_snap(sc, snapA)
                        S.inc("Ddbg", sc.dma_start(out=dxn_h[:, :],
                                                    in_=xn[:, :]), 16)

                nc.leave_named_scope("phA", _sc_A, False)
                if STOP_AFTER < 2:
                    raise _StopEmit()
                _sc_B = nc.enter_named_scope("phB", False)[0]
                with ExitStack() as bes:
                    fuse = bes.enter_context(
                        nc.sbuf_tensor("fuse", [128, 32 * 512], BF16))
                    wgb = [bes.enter_context(nc.sbuf_tensor(
                        "wgb%d" % i, [128, 2048], BF16)) for i in range(2)]
                    wub = [bes.enter_context(nc.sbuf_tensor(
                        "wub%d" % i, [128, 2048], BF16)) for i in range(2)]
                    wdb = [bes.enter_context(nc.sbuf_tensor(
                        "wdb%d" % i, [128, 4096], BF16)) for i in range(2)]
                    silu_t = [bes.enter_context(nc.sbuf_tensor(
                        "silu%d" % i, [128, 512], BF16)) for i in range(2)]
                    facc = bes.enter_context(
                        nc.sbuf_tensor("facc", [128, DP * 512], F32))
                    ffn_t = [bes.enter_context(nc.sbuf_tensor(
                        "ffnt%d" % i, [128, 512], F32)) for i in range(2)]
                    ps_g = [bes.enter_context(nc.psum_tensor(
                        "ps_g%d" % i, [128, 512], F32)) for i in range(2)]
                    ps_u = [bes.enter_context(nc.psum_tensor(
                        "ps_u%d" % i, [128, 512], F32)) for i in range(2)]
                    ps_dn = [bes.enter_context(nc.psum_tensor(
                        "ps_dn%d" % i, [128, 512], F32)) for i in range(2)]

                    for b in range(NB):
                        xoff = 1 + b * BL
                        for hf in range(2):
                            for mi in range(32):
                                m = hf * 32 + mi
                                i = (b * 2 + hf) * 32 + mi

                                @blk.gpsimd
                                def _(g, b=b, m=m, mi=mi, i=i):
                                    if i == 0:
                                        S.wait_snap(g, snapA)
                                    if i >= 2:
                                        S.w(g, "TE", cm.get(("bum", i - 2)))
                                    S.inc("Dbg%d" % (i % 2), g.dma_start(
                                        out=wgb[i % 2][:, :],
                                        in_=wgate8_h[m, :, :]), 16)
                                    cm[("bwl", i)] = S.dinc(
                                        "Dbg%d" % (i % 2), g.dma_start(
                                            out=wub[i % 2][:, :],
                                            in_=wup8_h[m, :, :]))

                                @blk.tensor
                                def _(te, b=b, m=m, mi=mi, i=i, xoff=xoff):
                                    S.wt(te, cm[("bwl", i)])
                                    if i == 0:
                                        S.wait_snap(te, snapA)
                                    if i >= 2:
                                        S.w(te, "SE", cm.get(("bsl", i - 2)))
                                    ii = None
                                    for d in range(DP):
                                        ii = te.matmul(
                                            ps_g[i % 2][:, :],
                                            wgb[i % 2][:, d * 128:(d + 1) * 128],
                                            xn[:, d * XNW + xoff:
                                               d * XNW + xoff + BL],
                                            start=(d == 0), stop=(d == DP - 1))
                                    cm[("bgm", i)] = S.inc("TE", ii, 1)
                                    if i >= 2:
                                        S.w(te, "VE", cm.get(("bfu", i - 2)))
                                    for d in range(DP):
                                        ii = te.matmul(
                                            ps_u[i % 2][:, :],
                                            wub[i % 2][:, d * 128:(d + 1) * 128],
                                            xn[:, d * XNW + xoff:
                                               d * XNW + xoff + BL],
                                            start=(d == 0), stop=(d == DP - 1))
                                    cm[("bum", i)] = S.inc("TE", ii, 1)

                                @blk.scalar
                                def _(sc, m=m, i=i):
                                    S.w(sc, "TE", cm[("bgm", i)])
                                    if i >= 2:
                                        S.w(sc, "VE", cm.get(("bfu", i - 2)))
                                    cm[("bsl", i)] = S.inc(
                                        "SE", sc.activation(
                                            silu_t[i % 2][:, :],
                                            ps_g[i % 2][:, :], AF.Silu,
                                            scale=sgT[:, m:m + 1]), 1)

                                @blk.vector
                                def _(v, m=m, mi=mi, i=i):
                                    S.w(v, "TE", cm[("bum", i)])
                                    S.w(v, "SE", cm[("bsl", i)])
                                    if i >= 1:
                                        S.w(v, "TE", cm.get(("bdm", (i // 32) - 1, DP - 1)))
                                    cm[("bfu", i)] = S.inc(
                                        "VE", v.scalar_tensor_tensor(
                                            fuse[:, mi * 512:(mi + 1) * 512],
                                            ps_u[i % 2][:, :],
                                            suT[:, m:m + 1],
                                            silu_t[i % 2][:, :],
                                            ALU.mult, ALU.mult), 1)

                            ihf = b * 2 + hf
                            for o in range(DP):
                                @blk.gpsimd
                                def _(g, hf=hf, o=o, ihf=ihf):
                                    j = ihf * DP + o
                                    if j >= 2:
                                        S.w(g, "TE", cm.get(("bdm2", j - 2)))
                                    cm[("bdl", j)] = S.dinc(
                                        "Dbd%d" % (j % 2), g.dma_start(
                                            out=wdb[j % 2][:, :],
                                            in_=wdn8_h[o, :, hf * 4096:
                                                       (hf + 1) * 4096]))

                                @blk.tensor
                                def _(te, b=b, hf=hf, o=o, ihf=ihf):
                                    j = ihf * DP + o
                                    S.wt(te, cm[("bdl", j)])
                                    if o == 0:
                                        S.w(te, "VE", cm[("bfu", ihf * 32 + 31)])
                                    if j >= 2:
                                        S.w(te, "VE", cm.get(("bfa", j - 2)))
                                    ii = None
                                    for mi in range(32):
                                        ii = te.matmul(
                                            ps_dn[j % 2][:, :],
                                            wdb[j % 2][:, mi * 128:(mi + 1) * 128],
                                            fuse[:, mi * 512:(mi + 1) * 512],
                                            start=(mi == 0), stop=(mi == 31))
                                    cm[("bdm", ihf, o)] = S.inc("TE", ii, 1)
                                    cm[("bdm2", j)] = cm[("bdm", ihf, o)]

                                @blk.vector
                                def _(v, b=b, hf=hf, o=o, ihf=ihf):
                                    j = ihf * DP + o
                                    S.w(v, "TE", cm[("bdm", ihf, o)])
                                    if hf == 0:
                                        ii = v.tensor_copy(
                                            facc[:, o * 512:(o + 1) * 512],
                                            ps_dn[j % 2][:, :])
                                    else:
                                        ii = v.tensor_add(
                                            facc[:, o * 512:(o + 1) * 512],
                                            facc[:, o * 512:(o + 1) * 512],
                                            ps_dn[j % 2][:, :])
                                    cm[("bfa", j)] = S.inc("VE", ii, 1)

                        for o in range(DP):
                            @blk.vector
                            def _(v, b=b, o=o):
                                S.w(v, "VE", cm[("bfa", (b * 2 + 1) * DP + o)])
                                jo = b * DP + o
                                if jo >= 2:
                                    S.wt(v, cm.get(("bst", jo - 2)))
                                cm[("bft", jo)] = S.inc(
                                    "VE", v.tensor_scalar_mul(
                                        ffn_t[jo % 2][:, :],
                                        facc[:, o * 512:(o + 1) * 512],
                                        sdT[:, o:o + 1]), 1)

                            @blk.scalar
                            def _(sc, b=b, o=o):
                                jo = b * DP + o
                                S.w(sc, "VE", cm[("bft", jo)])
                                cm[("bst", jo)] = S.dinc(
                                    "Dfs%d" % (jo % 2), sc.dma_start(
                                        out=ffn_s[o * 128:(o + 1) * 128,
                                                  b * BL:(b + 1) * BL],
                                        in_=ffn_t[jo % 2][:, :]))
                snapB = S.snap()


            nc.leave_named_scope("phB", _sc_B, False)
            if STOP_AFTER < 3:
                raise _StopEmit()
            with ExitStack() as ces:
                def CSB(n, s, d):
                    return ces.enter_context(nc.sbuf_tensor(n, s, d))

                def CPS(n, s, d=F32):
                    return ces.enter_context(nc.psum_tensor(n, s, d))

                e_sb = CSB("e_sb", [128, NG * EW], F32)
                zt1 = CSB("zt1", [128, 512], BF16)
                zt2 = CSB("zt2", [32, 512], BF16)
                xl = [CSB("xlA", [128, DP * 512], BF16),
                      CSB("xlB", [128, DP * 512], BF16)]
                xtm = [CSB("xtmA", [128, 512], F32),
                       CSB("xtmB", [128, 512], F32)]
                t2b = [CSB("t2A", [128, 512], F32)]
                r_sb = CSB("r_sb", [128, NG * 512], BF16)
                k_sb = CSB("k_sb", [128, NG * 512], BF16)
                tw_sb = CSB("tw_sb", [64, 512], BF16)
                wbuf = [CSB("wbA", [128, 512], BF16),
                        CSB("wbB", [128, 512], BF16)]
                wx2w = [CSB("wx2wA", [128, 2048], BF16)]
                vevst = [CSB("vevA", [128, 512], BF16),
                         CSB("vevB", [128, 512], BF16)]
                vwin = [CSB("vwinA", [64, 2048], BF16),
                        CSB("vwinB", [64, 2048], BF16)]
                rtst = CSB("rtst", [128, NG * 512], BF16)
                ktst = CSB("ktst", [128, NG * 512], BF16)
                srsta = CSB("srsta", [128, NG * 512], BF16)
                xcum = CSB("xcum", [128, NG * 64], F32)
                s2b = CSB("s2b", [128, NG], F32)
                u0b = CSB("u0b", [128, NG], F32)
                m0b = CSB("m0b", [128, NG], F32)
                m1b = CSB("m1b", [128, NG], F32)
                nm0b = CSB("nm0b", [128, NG], F32)
                nm1b = CSB("nm1b", [128, NG], F32)
                flab = CSB("flab", [128, NG], F32)
                ns2b = CSB("ns2b", [128, NG], F32)
                lamb = CSB("lamb", [128, NG], F32)
                cexpb = CSB("cexpb", [128, NG], F32)
                ninb = CSB("ninb", [128, NG], F32)
                mhf = CSB("mhf", [128, 512], F32)
                aer = CSB("aer", [128, 512], F32)
                aek = CSB("aek", [128, 512], F32)
                kfb = CSB("kfb", [128, 16], F32)
                khT_a = CSB("khT_a", [64, 512], BF16)

                ps_m = [CPS("ps_m%d" % i, [128, 512]) for i in range(2)]
                ps_p = [CPS("ps_p%d" % i, [128, 512]) for i in range(4)]
                psTa = CPS("psTa", [128, 512])
                ps_tra = CPS("ps_tra", [64, 512], BF16)

                @blk.gpsimd
                def _(g):
                    S.wait_snap(g, snapB)
                    last = None
                    for gg in range(NG):
                        last = g.memset(e_sb[:, gg * EW:gg * EW + 1], 0.0)
                    cm["einit"] = S.inc("GE", last, 1)

                for p in range(NB):
                    _sc_C = nc.enter_named_scope("phC%d" % p, False)[0]
                    emit_pass_C(nc, blk, S, cm, p, locals())
                    nc.leave_named_scope("phC%d" % p, _sc_C, False)

                snapC = S.snap()

                if STOP_AFTER < 4:
                    raise _StopEmit()
                @blk.scalar
                def _(sc):
                    S.wait_snap(sc, snapC)
                    S.inc("Dcc", sc.dma_start(out=cc_in[:, :],
                                               in_=slive[:, :]), 16)
                cm["ccst"] = ("Dcc", S.n("Dcc"))

                @blk.gpsimd
                def _(g):
                    S.wt(g, cm["ccst"])
                    ii = g.collective_compute(
                        "AllGather", ALU.bypass,
                        replica_groups=[[0, 1], [2, 3], [4, 5], [6, 7]],
                        ins=[cc_in[:, :].opt()],
                        outs=[cc_out[:, :].opt()])
                    S.inc("CCS", ii, 1)


        if STOP_AFTER < 5:
            raise _StopEmit()
        _sc_2A = nc.enter_named_scope("ph2A", False)[0]
        o_sbuf = SB("o_sbuf", [128, DP * NT], BF16)
        snapG = S.snap()
        with ExitStack() as a2:
            def A2SB(n, s, d):
                return a2.enter_context(nc.sbuf_tensor(n, s, d))

            rtw = [A2SB("rtw%d" % i, [128, NG * 512], BF16) for i in range(2)]
            ktw = [A2SB("ktw%d" % i, [128, NG * 512], BF16) for i in range(2)]
            srw = [A2SB("srw%d" % i, [128, NG * 512], BF16) for i in range(2)]
            vw2 = [A2SB("vw2%d" % i, [64, 2048], BF16) for i in range(2)]
            gtt_su = A2SB("gtt_su", [128, 64], F32)
            sublk = [A2SB("sublk%d" % i, [128, 128], BF16) for i in range(4)]
            rblk = [A2SB("rblk%d" % i, [128, 128], BF16) for i in range(3)]
            rL_sb = A2SB("rL_sb", [128, 32], BF16)
            kL_sb = A2SB("kL_sb", [128, 32], BF16)
            at_sb = [A2SB("at%d" % i, [64, 256], BF16) for i in range(2)]
            i64 = A2SB("i64", [64, 64], BF16)
            bm = A2SB("bm", [128, 128], BF16)
            pm0 = A2SB("pm0", [128, 128], BF16)
            pm1 = A2SB("pm1", [128, 128], BF16)
            s0 = A2SB("s0", [128, 512], F32)

            @blk.sync
            def _(q):
                S.wait_snap(q, snapG)
                S.inc("Da2", q.dma_start(out=i64[:, :], in_=i64_h[:, :]), 16)
                S.inc("Da2", q.dma_start(out=bm[:, :], in_=bm_h[:, :]), 16)
                S.inc("Da2", q.dma_start(out=pm0[:, :], in_=pm0_h[:, :]), 16)
                S.inc("Da2", q.dma_start(out=pm1[:, :], in_=pm1_h[:, :]), 16)
                S.w(q, "CCS", S.n("CCS"))
                S.inc("Da2", q.dma_start(out=s0[:, :],
                                         in_=cc_out[0:128, :]), 16)
            cm["s0ld"] = ("Da2", S.n("Da2"))
            if DEBUG:
                @blk.scalar
                def _(sc):
                    S.wt(sc, cm["s0ld"])
                    S.inc("Ddbg", sc.dma_start(out=ds0_h[:, :],
                                                in_=s0[:, :]), 16)
                    S.inc("Ddbg", sc.dma_start(out=dcoef_h[:, :],
                                               in_=coef[:, :]), 16)
                    S.inc("Ddbg", sc.dma_start(out=du_h[:, :],
                                               in_=u_all[:, :]), 16)
                    NN = NCH * NG
                    S.inc("Ddbg", sc.dma_start(out=dem_h[:, 0:NN],
                                               in_=em0s[:, :]), 16)
                    S.inc("Ddbg", sc.dma_start(out=dem_h[:, NN:2 * NN],
                                               in_=em1s[:, :]), 16)
                    S.inc("Ddbg", sc.dma_start(out=dem_h[:, 2 * NN:3 * NN],
                                               in_=fLrs[:, :]), 16)
            ps_att = [a2.enter_context(nc.psum_tensor(
                "ps_at%d" % i, [64, 384], F32)) for i in range(2)]
            ps_o = [a2.enter_context(nc.psum_tensor(
                "ps_o%d" % i, [128, 64], F32)) for i in range(4)]

            @blk.vector
            def _(v):
                S.wait_snap(v, snapG)
                v.memset(ps_att[0][:, :], 0.0)
                v.memset(ps_att[1][:, :], 0.0)
                v.memset(at_sb[0][:, :], 0.0)
                cm["2amz"] = S.inc("VE", v.memset(at_sb[1][:, :], 0.0), 1)

            for c in range(NCH):
                emit_chunk_2A(nc, blk, S, cm, c, locals(), snapG)
        snap2A = S.snap()
        if DEBUG:
            @blk.scalar
            def _(sc):
                S.wait_snap(sc, snap2A)
                S.inc("Ddbg", sc.dma_start(out=do_h[:, :],
                                            in_=o_sbuf[:, :]), 16)

        nc.leave_named_scope("ph2A", _sc_2A, False)
        if STOP_AFTER < 6:
            raise _StopEmit()
        _sc_2B = nc.enter_named_scope("ph2B", False)[0]
        with ExitStack() as b2:
            def B2SB(n, s, d):
                return b2.enter_context(nc.sbuf_tensor(n, s, d))

            indf = B2SB("indf", [128, 512], F32)
            indb = B2SB("indb", [128, 512], BF16)
            bcp = B2SB("bcp", [32, 2048], F32)
            xlgw = [B2SB("xlgw%d" % i, [128, DP * 512], BF16)
                    for i in range(2)]
            wgw = [B2SB("wgw%d" % i, [128, 1024], BF16) for i in range(2)]
            wow = [B2SB("wow%d" % i, [128, 1024], BF16) for i in range(2)]
            sgsb = B2SB("sgsb", [128, DP * 512], BF16)
            ogsb = B2SB("ogsb", [128, DP * 512], BF16)
            sqsb = [B2SB("sq2%d" % i, [128, 512], F32) for i in range(2)]
            stt2 = B2SB("stt2", [32, 4 * 512], F32)
            rst2 = B2SB("rst2", [32, 512], F32)
            gtt = [B2SB("gtt%d" % i, [128, 512], F32) for i in range(2)]
            xrd = [B2SB("xrd%d" % i, [128, 512], F32) for i in range(2)]
            frd = [B2SB("frd%d" % i, [128, 512], F32) for i in range(2)]
            ost = [B2SB("ost%d" % i, [128, 512], F32) for i in range(2)]

            @blk.sync
            def _(q):
                S.wait_snap(q, snap2A)
                S.inc("Db2", q.dma_start(out=indf[:, :],
                                          in_=indf_h[:, :]), 16)
                S.inc("Db2", q.dma_start(out=indb[:, :],
                                          in_=indb_h[:, :]), 16)
                S.inc("Db2", q.dma_start(out=bcp[:, :], in_=bcp_h[:, :]), 16)
            cm["c2bld"] = ("Db2", S.n("Db2"))

            for b in range(NB):
                emit_block_2B(nc, blk, S, cm, b, locals(), b2, snap2A)

        nc.leave_named_scope("ph2B", _sc_2B, False)
        endsnap = S.snap()

        @blk.sync
        def _(q):
            S.wait_snap(q, endsnap)
    except _StopEmit:
        pass
    return nc


def emit_pass_C(nc, blk, S, cm, p, L):
    xn = L["xn"]
    e_sb = L["e_sb"]
    xl, xtm, t2b = L["xl"], L["xtm"], L["t2b"]
    zt1, zt2 = L["zt1"], L["zt2"]
    r_sb, k_sb, tw_sb = L["r_sb"], L["k_sb"], L["tw_sb"]
    wbuf, wx2w = L["wbuf"], L["wx2w"]
    vevst, vwin = L["vevst"], L["vwin"]
    rtst, ktst, srsta = L["rtst"], L["ktst"], L["srsta"]
    ps_m, ps_p = L["ps_m"], L["ps_p"]
    psTa, ps_tra = L["psTa"], L["ps_tra"]
    ps_ua = psTa[0:64, 0:32]
    muT = L["muT"]
    bwT = L["bwT"]
    bonusT = L["bonusT"]
    wx1_sb, ww1_sb, ww2_sb = L["wx1_sb"], L["ww1_sb"], L["ww2_sb"]
    ones64 = L["ones64"]
    gateT = L["gateT"]
    grun, coef, u_all = L["grun"], L["coef"], L["u_all"]
    em0s, em1s, fLrs = L["em0s"], L["em1s"], L["fLrs"]
    slive = L["slive"]
    seg4, id128 = L["seg4"], L["id128"]
    xcum, s2b, ns2b = L["xcum"], L["s2b"], L["ns2b"]
    u0b, m0b, m1b = L["u0b"], L["m0b"], L["m1b"]
    nm0b, nm1b, flab = L["nm0b"], L["nm1b"], L["flab"]
    lamb, cexpb, ninb = L["lamb"], L["cexpb"], L["ninb"]
    mhf, aer, aek, kfb = L["mhf"], L["aer"], L["aek"], L["kfb"]
    khT_a = L["khT_a"]
    khata = aek.bitcast(BF16)[:, 0:512]
    bka = aer.bitcast(BF16)[:, 512:1024]
    p_a = aer.bitcast(BF16)[:, 0:512]
    wrb_s, wkb_s, wvb_s, wx2b_s = (L["wrb_s"], L["wkb_s"], L["wvb_s"],
                                   L["wx2b_s"])
    vtok_s, rt_s, kt_s, srec_s, xlg_s = (L["vtok_s"], L["rt_s"], L["kt_s"],
                                         L["srec_s"], L["xlg_s"])
    snapB = L["snapB"]
    DEBUGl = L["DEBUG"] if "DEBUG" in L else False

    xoff = 1 + p * BL

    @blk.vector
    def _(v):
        if p == 0:
            S.wait_snap(v, snapB)
            for mk in cm["precast"]:
                S.wt(v, mk)
            S.w(v, "GE", cm["einit"])
        else:
            for g in range(NG):
                v.tensor_copy(e_sb[:, g * EW:g * EW + 1],
                              e_sb[:, g * EW + BL:g * EW + BL + 1])
        cm[("bnd", p)] = S.inc("VE", v.engine_nop(), 1)

    @blk.sync
    def _(q):
        if p == 0:
            S.wait_snap(q, snapB)
            for mk in cm["precast"]:
                S.wt(q, mk)
        for proj in range(5):
            if (p, proj) != (0, 0):
                pp_, pj_ = (p, proj - 1) if proj >= 1 else (p - 1, 4)
                S.w(q, "TE", cm.get(("mml", pp_, pj_)))
            pr = 32 * (proj % 4)
            cm[("wx2l", p, proj)] = S.dinc("Dwx2", q.dma_start(
                out=wx2w[0][pr:pr + 32, :],
                in_=wx2b_s[:, proj * 2048:(proj + 1) * 2048]))

    for d in range(DP):
        @blk.vector
        def _(v, d=d):
            if d == 0:
                S.w(v, "VE", cm[("bnd", p)])
                if p > 0:
                    S.w(v, "TE", cm.get(("pjc", p - 1, 3)))
            cur = xn[:, d * XNW + xoff:d * XNW + xoff + BL]
            prv = xn[:, d * XNW + xoff - 1:d * XNW + xoff - 1 + BL]
            v.tensor_sub(xtm[d % 2][:, :], prv, cur)
            ii = v.scalar_tensor_tensor(
                xl[0][:, d * 512:(d + 1) * 512], xtm[d % 2][:, :],
                muT[:, d:d + 1], cur, ALU.mult, ALU.add)
            cm[("zxl", p, d)] = S.inc("VE", ii, 1)

    for d in range(DP):
        @blk.tensor
        def _(te, d=d):
            if d == 0:
                S.wt(te, cm["smallw"])
                if p == 0:
                    S.wait_snap(te, snapB)
                else:
                    S.w(te, "SE", cm.get(("zev", p - 1)))
                    S.w(te, "VE", cm.get(("vevl", p - 1)))
                    S.w(te, "SE", cm.get(("eev", p - 1, NG - 1)))
            S.w(te, "VE", cm[("zxl", p, d)])
            te.matmul(ps_p[0][:, :], wx1_sb[:, d * 160:d * 160 + 128],
                      xl[0][:, d * 512:(d + 1) * 512],
                      start=(d == 0), stop=(d == DP - 1))
            ii = te.matmul(ps_p[1][0:32, :],
                           wx1_sb[:, d * 160 + 128:d * 160 + 160],
                           xl[0][:, d * 512:(d + 1) * 512],
                           start=(d == 0), stop=(d == DP - 1))
            cm[("zmm", p, d)] = S.inc("TE", ii, 1)

    @blk.scalar
    def _(sc):
        S.w(sc, "TE", cm[("zmm", p, DP - 1)])
        sc.activation(zt1[:, :], ps_p[0][:, :], AF.Tanh)
        cm[("zev", p)] = S.inc("SE", sc.activation(
            zt2[:, :], ps_p[1][0:32, :], AF.Tanh), 1)

    for proj in range(5):
        dst = xl[(proj + 1) % 2]

        @blk.sync
        def _(q, proj=proj):
            if (p, proj) != (0, 0):
                prevk = ("mml", p, proj - 1) if proj >= 1 else \
                    ("mml", p - 1, 4)
                S.w(q, "TE", cm.get(prevk))
            pr = 32 * (proj % 4)
            cm[("wx2l", p, proj)] = S.dinc("Dwx2", q.dma_start(
                out=wx2w[0][pr:pr + 32, :],
                in_=wx2b_s[:, proj * 2048:(proj + 1) * 2048]))

        for d in range(DP):
            @blk.tensor
            def _(te, proj=proj, d=d):
                if d == 0:
                    S.wt(te, cm[("wx2l", p, proj)])
                    S.w(te, "SE", cm[("zev", p)])
                zsrc = (zt1[32 * proj:32 * proj + 32, :] if proj < 4
                        else zt2[:, :])
                tp = (32 * proj, 0) if proj < 4 else (0, 0)
                key = ("t2c", p, proj, d - 2)
                if d < 2:
                    key = ("t2c", p, proj - 1, DP - 2 + d) if proj > 0 else \
                        (("t2c", p - 1, 4, DP - 2 + d) if p > 0 else None)
                if key:
                    S.w(te, "VE", cm.get(key))
                pr = 32 * (proj % 4)
                ii = te.matmul(
                    ps_m[d % 2][:, :],
                    wx2w[0][pr:pr + 32, d * 128:(d + 1) * 128],
                    zsrc, start=True, stop=True, tile_position=tp)
                cm[("mmi", p, proj, d)] = S.inc("TE", ii, 1)
                if d == DP - 1:
                    cm[("mml", p, proj)] = cm[("mmi", p, proj, d)]

            @blk.vector
            def _(v, proj=proj, d=d, dst=dst):
                if d == 0:
                    if proj == 0:
                        if p > 0:
                            S.w(v, "GE", cm.get(("xlg_done", p - 1)))
                    elif proj == 1:
                        S.w(v, "TE", cm.get(("zmm", p, DP - 1)))
                    else:
                        S.w(v, "TE", cm.get(("pjc", p, proj - 2)))
                cur = xn[:, d * XNW + xoff:d * XNW + xoff + BL]
                prv = xn[:, d * XNW + xoff - 1:d * XNW + xoff - 1 + BL]
                v.tensor_sub(xtm[d % 2][:, :], prv, cur)
                pkey = ("xlr", p, proj, d - 1)
                if d < 1:
                    pkey = (("xlr", p, proj - 1, DP - 1) if proj > 0
                            else (("xlr", p - 1, 4, DP - 1) if p > 0
                                  else None))
                if pkey:
                    S.w(v, "GE", cm.get(pkey))
                S.w(v, "TE", cm[("mmi", p, proj, d)])
                ii = v.tensor_mul(t2b[0][:, :], xtm[d % 2][:, :],
                                  ps_m[d % 2][:, :])
                cm[("t2c", p, proj, d)] = S.inc("VE", ii, 1)

            @blk.gpsimd
            def _(gp, proj=proj, d=d, dst=dst):
                S.w(gp, "VE", cm[("t2c", p, proj, d)])
                cur = xn[:, d * XNW + xoff:d * XNW + xoff + BL]
                ii = gp.tensor_add(dst[:, d * 512:(d + 1) * 512],
                                   t2b[0][:, :], cur)
                cm[("xlr", p, proj, d)] = S.inc("GE", ii, 1)

        if proj in (0, 2):
            wsrc = wrb_s if proj == 0 else wkb_s
            osb = r_sb if proj == 0 else k_sb
            for mg in range(2):
                for d in range(DP):
                    i = mg * DP + d

                    @blk.sync
                    def _(q, proj=proj, wsrc=wsrc, mg=mg, d=d, i=i):
                        if i >= 2:
                            S.w(q, "TE", cm.get(("pwm", p, proj, i - 2)))
                        elif proj == 2:
                            S.w(q, "TE", cm.get(("pwm", p, 0, 2 * DP - 2 + i)))
                        elif proj == 0 and p > 0:
                            S.w(q, "TE", cm.get(("vwm", p - 1, 4 * DP - 2 + i)))
                        cm[("pwl", p, proj, i)] = S.dinc(
                            "Dwb%d" % (i % 2), q.dma_start(
                                out=wbuf[i % 2][:, :],
                                in_=wsrc[d * 128:(d + 1) * 128,
                                         mg * 512:(mg + 1) * 512]))

                    @blk.tensor
                    def _(te, proj=proj, mg=mg, d=d, i=i):
                        S.wt(te, cm[("pwl", p, proj, i)])
                        S.w(te, "GE", cm[("xlr", p, proj, d)])
                        if d == 0 and mg == 1:
                            S.w(te, "VE", cm[("pev", p, proj, 0)])
                        if d == 0 and mg == 0:
                            key = ("pev", p, proj - 2, 1) if proj == 2 else \
                                (("vevl", p - 1) if p > 0 else None)
                            if key:
                                S.w(te, "VE", cm.get(key))
                        last = None
                        for m in range(4):
                            last = te.matmul(
                                ps_p[m][:, :],
                                wbuf[i % 2][:, m * 128:(m + 1) * 128],
                                xl[(proj + 1) % 2][:, d * 512:(d + 1) * 512],
                                start=(d == 0), stop=(d == DP - 1))
                        cm[("pwm", p, proj, i)] = S.inc("TE", last, 1)

                @blk.vector
                def _(v, proj=proj, osb=osb, mg=mg):
                    S.w(v, "TE", cm[("pwm", p, proj, mg * DP + DP - 1)])
                    last = None
                    for m in range(4):
                        g = mg * 4 + m
                        last = v.tensor_copy(osb[:, g * 512:(g + 1) * 512],
                                             ps_p[m][:, :])
                    cm[("pev", p, proj, mg)] = S.inc("VE", last, 1)
            cm[("pjc", p, proj)] = cm[("pwm", p, proj, 2 * DP - 1)]
        elif proj == 1:
            for d in range(DP):
                @blk.tensor
                def _(te, d=d):
                    if d == 0:
                        key = ("pev", p, 0, 1)
                        S.w(te, "VE", cm.get(key))
                    S.w(te, "GE", cm[("xlr", p, 1, d)])
                    ii = te.matmul(ps_p[0][0:64, :],
                                   ww1_sb[:, d * 64:(d + 1) * 64],
                                   xl[0][:, d * 512:(d + 1) * 512],
                                   start=(d == 0), stop=(d == DP - 1))
                    cm[("twm", p, d)] = S.inc("TE", ii, 1)

            @blk.scalar
            def _(sc):
                S.w(sc, "TE", cm[("twm", p, DP - 1)])
                S.w(sc, "VE", cm[("bnd", p)])
                cm[("twe", p)] = S.inc("SE", sc.activation(
                    tw_sb[:, :], ps_p[0][0:64, :], AF.Tanh), 1)

            for m in range(NG):
                @blk.tensor
                def _(te, m=m):
                    if m == 0:
                        S.w(te, "SE", cm[("twe", p)])
                    if m >= 2:
                        S.w(te, "SE", cm.get(("eev", p, m - 2)))
                    elif p > 0 and m < 2:
                        S.w(te, "SE", cm.get(("eev", p - 1, NG - 2 + m)))
                    ii = te.matmul(ps_p[1 + m % 2][:, :],
                                   ww2_sb[:, m * 128:(m + 1) * 128],
                                   tw_sb[:, :], start=True, stop=True)
                    cm[("wmm", p, m)] = S.inc("TE", ii, 1)

                @blk.scalar
                def _(sc, m=m):
                    S.w(sc, "TE", cm[("wmm", p, m)])
                    if p > 0 and m == 0:
                        S.w(sc, "VE", cm.get(("scandone", p - 1)))
                    ii = sc.activation(
                        e_sb[:, m * EW + 1:m * EW + 1 + BL],
                        ps_p[1 + m % 2][:, :], AF.Exp, bias=bwT[:, m:m + 1])
                    cm[("eev", p, m)] = S.inc("SE", ii, 1)
            cm[("pjc", p, 1)] = cm[("twm", p, DP - 1)]
        elif proj == 3:
            for dvc in range(4):
                for d in range(DP):
                    i = dvc * DP + d

                    @blk.sync
                    def _(q, dvc=dvc, d=d, i=i):
                        if i >= 2:
                            S.w(q, "TE", cm.get(("vwm", p, i - 2)))
                        else:
                            S.w(q, "TE", cm.get(("pwm", p, 2, 2 * DP - 2 + i)))
                        cm[("vwl", p, i)] = S.dinc(
                            "Dwb%d" % (i % 2), q.dma_start(
                                out=wbuf[i % 2][:, :],
                                in_=wvb_s[d * 128:(d + 1) * 128,
                                          dvc * 512:(dvc + 1) * 512]))

                    @blk.tensor
                    def _(te, dvc=dvc, d=d, i=i):
                        S.wt(te, cm[("vwl", p, i)])
                        S.w(te, "GE", cm[("xlr", p, 3, d)])
                        if d == 0:
                            if dvc == 0:
                                S.w(te, "VE", cm.get(("pev", p, 2, 1)))
                            else:
                                S.w(te, "VE", cm.get(("vev", p, dvc - 1)))
                        last = None
                        for t in range(4):
                            last = te.matmul(
                                ps_p[t][:, :],
                                xl[0][:, d * 512 + t * 128:
                                      d * 512 + (t + 1) * 128],
                                wbuf[i % 2][:, :],
                                start=(d == 0), stop=(d == DP - 1))
                        cm[("vwm", p, i)] = S.inc("TE", last, 1)

                for t in range(4):
                    @blk.vector
                    def _(v, dvc=dvc, t=t):
                        if t == 0:
                            S.w(v, "TE", cm[("vwm", p, dvc * DP + DP - 1)])
                        j = dvc * 4 + t
                        if j >= 2:
                            S.wt(v, cm.get(("vst", p, j - 2)))
                        ii = v.tensor_copy(vevst[j % 2][:, :], ps_p[t][:, :])
                        cm[("vevt", p, j)] = S.inc("VE", ii, 1)

                    @blk.scalar
                    def _(sc, dvc=dvc, t=t):
                        j = dvc * 4 + t
                        S.w(sc, "VE", cm[("vevt", p, j)])
                        c0 = p * 8 + 2 * t
                        S.inc("Dvs%d" % (j % 2), sc.dma_start(
                            out=vtok_s[c0, :, dvc * 512:(dvc + 1) * 512],
                            in_=vevst[j % 2][0:64, :]), 16)
                        ii = sc.dma_start(
                            out=vtok_s[c0 + 1, :, dvc * 512:(dvc + 1) * 512],
                            in_=vevst[j % 2][64:128, :])
                        cm[("vst", p, j)] = S.dinc("Dvs%d" % (j % 2), ii)

                @blk.vector
                def _(v, dvc=dvc):
                    cm[("vev", p, dvc)] = cm[("vevt", p, dvc * 4 + 3)]
            cm[("pjc", p, 3)] = cm[("vwm", p, 4 * DP - 1)]
            cm[("vevl", p)] = cm[("vevt", p, 15)]
        else:
            for d in range(DP):
                @blk.scalar
                def _(sc, d=d):
                    S.w(sc, "GE", cm[("xlr", p, 4, d)])
                    ii = sc.dma_start(
                        out=xlg_s[d, :, p * BL:(p + 1) * BL],
                        in_=xl[1][:, d * 512:(d + 1) * 512])
                    cm[("gst", p, d)] = S.dinc("Dgst", ii)

            @blk.gpsimd
            def _(gp):
                S.wt(gp, cm[("gst", p, DP - 1)])
                cm[("xlg_done", p)] = S.inc("GE", gp.engine_nop(), 1)
            cm[("pjc", p, 4)] = 0

    def b8(t, n=64):
        return t[:, :].rearrange("p (g a) -> p g a", a=1).broadcast_to(
            [128, NG, n])

    def g3(t, w, cl, n=64):
        return t[:, :].rearrange("p (g w) -> p g w", w=w)[
            :, :, cl * 64:cl * 64 + n]

    for cl in range(8):
        cglob = p * 8 + cl

        @blk.sync
        def _(q, cl=cl, cglob=cglob):
            S.wt(q, cm[("vst", p, 12 + cl // 2)])
            if cglob >= 2:
                S.w(q, "TE", cm.get(("tmma", cglob - 2)))
            cm[("vwl2", cglob)] = S.dinc(
                "Dvw%d" % (cl % 2), q.dma_start(
                    out=vwin[cl % 2][:, :], in_=vtok_s[cglob, :, :]))

        @blk.vector
        def _(v, cl=cl, cglob=cglob):
            if cglob > 0:
                S.w(v, "SE", cm.get(("seba", cglob - 1)))
            if cl == 0:
                for g in range(NG):
                    S.w(v, "SE", cm[("eev", p, g)])
                S.wt(v, cm.get(("cst", p - 1)))
                S.w(v, "GE", cm["ginit"])
            S.fence(v, "VE", v.tensor_scalar_mul(
                ninb[:, :], e_sb[:, cl * 64::EW], -1.0))
            last = None
            for g in range(NG):
                last = v.tensor_tensor_scan(
                    xcum[:, g * 64:(g + 1) * 64],
                    e_sb[:, g * EW + cl * 64:g * EW + cl * 64 + 64],
                    ones64[:, :], ninb[:, g:g + 1], ALU.add, ALU.mult)
            S.fence(v, "VE", last)
            v.tensor_add(s2b[:, :], xcum[:, 63::64],
                         e_sb[:, cl * 64 + 64::EW])
            S.fence(v, "VE", v.tensor_scalar_mul(
                m0b[:, :], xcum[:, 32::64], 0.5))
            v.tensor_scalar_mul(nm0b[:, :], m0b[:, :], -1.0)
            v.tensor_scalar_mul(ns2b[:, :], s2b[:, :], -1.0)
            S.fence(v, "VE", v.tensor_scalar_mul(
                ninb[:, :], s2b[:, :], 0.5))
            v.tensor_add(m1b[:, :], m0b[:, :], ninb[:, :])
            S.fence(v, "VE", v.tensor_sub(
                flab[:, :], m0b[:, :], ninb[:, :]))
            v.tensor_scalar_mul(nm1b[:, :], m1b[:, :], -1.0)
            v.tensor_sub(kfb[:, 0:8], m0b[:, :], s2b[:, :])
            v.tensor_sub(kfb[:, 8:16], m1b[:, :], s2b[:, :])
            m3 = mhf[:, :].rearrange("p (g t) -> p g t", t=64)
            v.tensor_copy(m3[:, :, 0:32], b8(m0b, 32))
            S.fence(v, "VE", v.tensor_copy(m3[:, :, 32:64], b8(m1b, 32)))
            xc3 = xcum[:, :].rearrange("p (g t) -> p g t", t=64)
            S.fence(v, "VE", v.tensor_sub(
                aer[:, :].rearrange("p (g t) -> p g t", t=64), m3, xc3))
            ii = v.tensor_sub(
                aek[:, :].rearrange("p (g t) -> p g t", t=64),
                e_sb[:, :].rearrange("p (g w) -> p g w", w=EW)[
                    :, :, cl * 64 + 1:cl * 64 + 65],
                aer[:, :].rearrange("p (g t) -> p g t", t=64))
            cm[("sc1", cglob)] = S.inc("VE", ii, 1)

        @blk.scalar
        def _(sc, cl=cl, cglob=cglob):
            S.w(sc, "VE", cm[("sc1", cglob)])
            if cglob > 0:
                S.w(sc, "VE", cm.get(("supa", cglob - 1)))
                S.w(sc, "TE", cm.get(("umma", cglob - 1)))
                S.w(sc, "TE", cm.get(("trmb", cglob - 1)))
            sc.activation(lamb[:, :], ns2b[:, :], AF.Exp)
            sc.activation(cexpb[:, :], grun[:, :], AF.Exp, scale=-1.0)
            sc.activation(em0s[:, cglob * NG:(cglob + 1) * NG],
                          nm0b[:, :], AF.Exp)
            sc.activation(em1s[:, cglob * NG:(cglob + 1) * NG],
                          nm1b[:, :], AF.Exp)
            sc.activation(fLrs[:, cglob * NG:(cglob + 1) * NG],
                          flab[:, :], AF.Exp)
            sc.activation(kfb[:, :], kfb[:, :], AF.Exp)
            sc.activation(aer[:, :], aer[:, :], AF.Exp)
            ii = sc.activation(aek[:, :], aek[:, :], AF.Exp)
            cm[("seba", cglob)] = S.inc("SE", ii, 1)

        @blk.vector
        def _(v, cl=cl, cglob=cglob):
            S.w(v, "SE", cm[("seba", cglob)])
            v.tensor_scalar_mul(coef[:, cglob * NG:(cglob + 1) * NG],
                                cexpb[:, :], gateT[:, 0:1])
            ii = v.tensor_add(grun[:, :], grun[:, :], s2b[:, :])
            cm[("sc2", cglob)] = S.inc("VE", ii, 1)

        @blk.vector
        def _(v, cl=cl, cglob=cglob):
            S.w(v, "SE", cm[("seba", cglob)])
            if cl == 0 and p > 0:
                S.wt(v, cm.get(("rkst", p - 1)))
            er3 = aer[:, :].rearrange("p (g t) -> p g t", t=64)
            ek3 = aek[:, :].rearrange("p (g t) -> p g t", t=64)
            v.tensor_mul(g3(rtst, 512, cl), g3(r_sb, 512, cl), er3)
            v.tensor_mul(g3(ktst, 512, cl), g3(k_sb, 512, cl), ek3)
            kf3 = mhf[:, :].rearrange("p (g t) -> p g t", t=64)
            v.tensor_copy(kf3[:, :, 0:32], b8(kfb[:, 0:8], 32))
            S.fence(v, "VE", v.tensor_copy(kf3[:, :, 32:64],
                                           b8(kfb[:, 8:16], 32)))
            ii = v.tensor_mul(
                khata[:, :].rearrange("p (g t) -> p g t", t=64),
                g3(ktst, 512, cl), kf3)
            cm[("khwa", cglob)] = S.inc("VE", ii, 1)
            v.tensor_mul(
                bka[:, :].rearrange("p (g t) -> p g t", t=64),
                g3(k_sb, 512, cl), b8(bonusT, 64))
            ii = v.tensor_mul(
                p_a[:, :].rearrange("p (g t) -> p g t", t=64),
                g3(r_sb, 512, cl),
                bka[:, :].rearrange("p (g t) -> p g t", t=64))
            cm[("pwa", cglob)] = S.inc("VE", ii, 1)

        @blk.tensor
        def _(te, cl=cl, cglob=cglob):
            S.w(te, "VE", cm[("pwa", cglob)])
            if cglob > 0:
                S.w(te, "VE", cm.get(("khTa", cglob - 1)))
                S.w(te, "VE", cm.get(("supa", cglob - 1)))
            ii = None
            for g in range(NG):
                ii = te.matmul(ps_ua[:, 4 * g:4 * g + 4],
                               p_a[:, g * 64:(g + 1) * 64], seg4[:, :],
                               start=True, stop=True)
            cm[("umma", cglob)] = S.inc("TE", ii, 1)

        @blk.tensor
        def _(te, cl=cl, cglob=cglob):
            S.w(te, "VE", cm[("khwa", cglob)])
            if cglob > 0:
                S.w(te, "VE", cm.get(("khTb", cglob - 1)))
            ii = None
            for g in range(4):
                ii = te.transpose(ps_tra[:, g * 128:(g + 1) * 128],
                                  khata[:, g * 64:(g + 1) * 64],
                                  id128[:, :])
            cm[("trma", cglob)] = S.inc("TE", ii, 1)

        @blk.vector
        def _(v, cl=cl, cglob=cglob):
            S.w(v, "TE", cm[("trma", cglob)])
            S.w(v, "TE", cm[("umma", cglob)])
            if cglob > 0:
                S.w(v, "TE", cm.get(("tmma", cglob - 1)))
            ua3 = u_all[:, :].rearrange("p (g x) -> p g x", x=NCH * 4)[
                :, :, cglob * 4:cglob * 4 + 4]
            v.tensor_copy(ua3, ps_ua[:, :].rearrange(
                "p (g f) -> p g f", f=4))
            ii = v.tensor_copy(khT_a[:, :], ps_tra[:, :])
            cm[("khTa", cglob)] = S.inc("VE", ii, 1)

        @blk.tensor
        def _(te, cl=cl, cglob=cglob):
            S.w(te, "VE", cm[("khTa", cglob)])
            S.wt(te, cm[("vwl2", cglob)])
            if cglob > 0:
                S.w(te, "VE", cm.get(("supa", cglob - 1)))
            last = None
            for g in range(4):
                for h in range(4):
                    last = te.matmul(
                        psTa[32 * h:32 * h + 32, g * 64:(g + 1) * 64],
                        khT_a[:, g * 128 + 32 * h:g * 128 + 32 * h + 32],
                        vwin[cl % 2][:, (g * 4 + h) * 64:
                                     (g * 4 + h) * 64 + 64],
                        start=True, stop=True,
                        tile_position=(0, 32 * h))
            cm[("tmmha", cglob)] = S.inc("TE", last, 1)
            ii = None
            for g in range(4, NG):
                ii = te.transpose(ps_tra[:, (g - 4) * 128:(g - 3) * 128],
                                  khata[:, g * 64:(g + 1) * 64],
                                  id128[:, :])
            cm[("trmb", cglob)] = S.inc("TE", ii, 1)

        @blk.vector
        def _(v, cl=cl, cglob=cglob):
            S.w(v, "TE", cm[("trmb", cglob)])
            S.w(v, "TE", cm[("tmmha", cglob)])
            ii = v.tensor_copy(khT_a[:, :], ps_tra[:, :])
            cm[("khTb", cglob)] = S.inc("VE", ii, 1)

        @blk.tensor
        def _(te, cl=cl, cglob=cglob):
            S.w(te, "VE", cm[("khTb", cglob)])
            last = None
            for g in range(4, NG):
                for h in range(4):
                    last = te.matmul(
                        psTa[32 * h:32 * h + 32, g * 64:(g + 1) * 64],
                        khT_a[:, (g - 4) * 128 + 32 * h:
                              (g - 4) * 128 + 32 * h + 32],
                        vwin[cl % 2][:, (g * 4 + h) * 64:
                                     (g * 4 + h) * 64 + 64],
                        start=True, stop=True,
                        tile_position=(0, 32 * h))
            cm[("tmma", cglob)] = S.inc("TE", last, 1)

        @blk.vector
        def _(v, cl=cl, cglob=cglob):
            v.tensor_copy(g3(srsta, 512, cl), slive[:, :].rearrange(
                "p (g t) -> p g t", t=64))
            S.fence(v, "VE", v.tensor_mul(
                slive[:, :].rearrange("p (g t) -> p g t", t=64),
                slive[:, :].rearrange("p (g t) -> p g t", t=64),
                b8(lamb, 64)))
            S.w(v, "TE", cm[("tmma", cglob)])
            ii = v.tensor_add(slive[:, :], slive[:, :], psTa[:, :])
            cm[("supa", cglob)] = S.inc("VE", ii, 1)

    @blk.vector
    def _(v):
        cm[("scandone", p)] = cm[("sc2", p * 8 + 7)]

    @blk.scalar
    def _(sc):
        S.w(sc, "VE", cm[("khwa", p * 8 + 7)])
        for g in range(NG):
            S.inc("Drk", sc.dma_start(
                out=rt_s[g, :, p * BL:(p + 1) * BL],
                in_=rtst[:, g * 512:(g + 1) * 512]), 16)
            S.inc("Drk", sc.dma_start(
                out=kt_s[g, :, p * BL:(p + 1) * BL],
                in_=ktst[:, g * 512:(g + 1) * 512]), 16)
        cm[("rkst", p)] = ("Drk", S.n("Drk"))
        S.w(sc, "VE", cm[("supa", p * 8 + 7)])
        for g in range(NG):
            S.inc("Dsr", sc.dma_start(
                out=srec_s[g, :, p * 8 * 64:(p + 1) * 8 * 64],
                in_=srsta[:, g * 512:(g + 1) * 512]), 16)
        cm[("cst", p)] = ("Dsr", S.n("Dsr"))
    if L.get("DEBUG_ON"):
        pass

    if DEBUG:
        @blk.scalar
        def _(sc):
            S.w(sc, "VE", cm[("supa", p * 8 + 7)])
            de_h = L["de_h"]
            dr_h = L["dr_h"]
            S.inc("Ddbg", sc.dma_start(out=de_h[p, :, :],
                                        in_=e_sb[:, :]), 16)
            S.inc("Ddbg", sc.dma_start(out=dr_h[p, :, :],
                                       in_=r_sb[:, :]), 16)


def emit_chunk_2A(nc, blk, S, cm, c, L, snapG):
    rtw, ktw, srw, vw2 = L["rtw"], L["ktw"], L["srw"], L["vw2"]
    gtt_su, sublk, rblk = L["gtt_su"], L["sublk"], L["rblk"]
    rL_sb, kL_sb, at_sb = L["rL_sb"], L["kL_sb"], L["at_sb"]
    ps_att, ps_o = L["ps_att"], L["ps_o"]
    o_sbuf = L["o_sbuf"]
    s0, coef, u_all = L["s0"], L["coef"], L["u_all"]
    em0s, em1s, fLrs = L["em0s"], L["em1s"], L["fLrs"]
    i64, bm, pm0, pm1 = L["i64"], L["bm"], L["pm0"], L["pm1"]
    rt_s, kt_s, srec_s, vtok_s = L["rt_s"], L["kt_s"], L["srec_s"], L["vtok_s"]
    ALUm, ALUa = ALU.mult, ALU.add
    w = c // 8

    if c % 8 == 0:
        @blk.sync
        def _(q, w=w):
            if c == 0:
                S.wait_snap(q, snapG)
            if w >= 2:
                S.w(q, "TE", cm.get(("2aom", (w - 2) * 8 + 7, NG - 1)))
            sl = "Daw%d" % (w % 2)
            for g in range(NG):
                S.inc(sl, q.dma_start(
                    out=rtw[w % 2][:, g * 512:(g + 1) * 512],
                    in_=rt_s[g, :, w * 512:(w + 1) * 512]), 16)
                S.inc(sl, q.dma_start(
                    out=ktw[w % 2][:, g * 512:(g + 1) * 512],
                    in_=kt_s[g, :, w * 512:(w + 1) * 512]), 16)
                S.inc(sl, q.dma_start(
                    out=srw[w % 2][:, g * 512:(g + 1) * 512],
                    in_=srec_s[g, :, w * 512:(w + 1) * 512]), 16)
            cm[("2awl", w)] = (sl, S.n(sl))

    @blk.sync
    def _(q, c=c):
        if c >= 2:
            S.w(q, "TE", cm.get(("2aom", c - 2, NG - 1)))
        cm[("2avl", c)] = S.dinc("Dav%d" % (c % 2), q.dma_start(
            out=vw2[c % 2][:, :], in_=vtok_s[c, :, :]))

    for g in range(NG):
        i2 = (c * NG + g) % 2
        io = (c * NG + g) % 2
        cc = c * NG + g
        co = g * 512 + (c % 8) * 64

        @blk.vector
        def _(v, c=c, g=g, i2=i2, w=w, cc=cc, co=co):
            if c == 0 and g == 0:
                S.wait_snap(v, snapG)
                S.wt(v, cm["s0ld"])
            S.wt(v, cm[("2awl", w)])
            if not (c == 0 and g == 0):
                pk = (c, g - 1) if g > 0 else (c - 1, NG - 1)
                S.w(v, "TE", cm.get(("2aom", pk[0], pk[1])))
            v.scalar_tensor_tensor(
                gtt_su[:, :], s0[:, g * 64:(g + 1) * 64],
                coef[:, cc:cc + 1],
                srw[w % 2][:, g * 512 + (c % 8) * 64:
                           g * 512 + (c % 8) * 64 + 64],
                ALUm, ALUa)
            v.tensor_scalar_mul(rL_sb[:, :], rtw[w % 2][:, co + 32:co + 64],
                                fLrs[:, cc:cc + 1])
            S.fence(v, "VE", v.tensor_scalar_mul(
                kL_sb[:, :], ktw[w % 2][:, co:co + 32],
                em0s[:, cc:cc + 1]))
            gbc = gtt_su[:, :].rearrange("p (a v) -> p a v", a=1).broadcast_to(
                [128, 2, 64])
            for pp in range(2):
                pmv = (pm0 if pp == 0 else pm1)[:, :].rearrange(
                    "p (a v) -> p a v", v=64)
                v.scalar_tensor_tensor(
                    sublk[2 * pp][:, :].rearrange("p (a v) -> p a v", v=64),
                    gbc, em0s[:, cc:cc + 1], pmv, ALUm, ALUm)
                v.scalar_tensor_tensor(
                    sublk[2 * pp + 1][:, :].rearrange("p (a v) -> p a v",
                                                      v=64),
                    gbc, em1s[:, cc:cc + 1], pmv, ALUm, ALUm)
            bmv = bm[:, :].rearrange("p (a t) -> p a t", t=32)
            for i, srcv in enumerate([rtw[w % 2][:, co:co + 32],
                                      rtw[w % 2][:, co + 32:co + 64],
                                      rL_sb[:, :]]):
                bc = srcv.rearrange("p (a t) -> p a t", a=1).broadcast_to(
                    [128, 4, 32])
                ii = v.tensor_mul(
                    rblk[i][:, :].rearrange("p (a t) -> p a t", t=32),
                    bc, bmv)
            cm[("2asu", c, g)] = S.inc("VE", ii, 1)

        @blk.tensor
        def _(te, c=c, g=g, i2=i2, w=w, co=co):
            if c == 0 and g == 0:
                S.wait_snap(te, snapG)
            S.wt(te, cm[("2awl", w)])
            S.w(te, "VE", cm[("2asu", c, g)])
            if c * NG + g >= 2:
                pi = c * NG + g - 2
                S.w(te, "VE", cm.get(("2aae", pi // NG, pi % NG)))
            else:
                S.w(te, "VE", cm["2amz"])
            te.matmul(ps_att[i2][0:32, 0:128], ktw[w % 2][:, co:co + 32],
                      rblk[0][:, :], start=True, stop=True)
            te.matmul(ps_att[i2][32:64, 128:256],
                      ktw[w % 2][:, co + 32:co + 64],
                      rblk[1][:, :], start=True, stop=True,
                      tile_position=(0, 32))
            ii = te.matmul(ps_att[i2][0:32, 256:384], kL_sb[:, :],
                           rblk[2][:, :], start=True, stop=True)
            cm[("2aam", c, g)] = S.inc("TE", ii, 1)

        @blk.vector
        def _(v, c=c, g=g, i2=i2):
            S.w(v, "TE", cm[("2aam", c, g)])
            if c * NG + g >= 2:
                pi = c * NG + g - 2
                S.w(v, "TE", cm.get(("2aom", pi // NG, pi % NG)))
            else:
                S.w(v, "VE", cm["2amz"])
            v.tensor_copy(
                at_sb[i2][0:32, :].rearrange("p (h t) -> p h t", t=64)[
                    :, :, 0:32],
                ps_att[i2][0:32, 0:128].rearrange("p (h t) -> p h t", t=32))
            v.tensor_copy(
                at_sb[i2][32:64, :].rearrange("p (h t) -> p h t", t=64)[
                    :, :, 32:64],
                ps_att[i2][32:64, 128:256].rearrange("p (h t) -> p h t",
                                                     t=32))
            ii = v.tensor_copy(
                at_sb[i2][0:32, :].rearrange("p (h t) -> p h t", t=64)[
                    :, :, 32:64],
                ps_att[i2][0:32, 256:384].rearrange("p (h t) -> p h t",
                                                    t=32))
            cm[("2aae", c, g)] = S.inc("VE", ii, 1)

        @blk.gpsimd
        def _(gp, c=c, g=g, i2=i2):
            S.w(gp, "VE", cm[("2aae", c, g)])
            ii = gp.affine_select(
                out=at_sb[i2][:, :].rearrange("p (b t) -> p b t", t=64),
                in_=at_sb[i2][:, :].rearrange("p (b t) -> p b t", t=64),
                compare_op=ALU.is_gt, fill=0.0, base=0,
                pattern=[[0, 4], [1, 64]], channel_multiplier=-1)
            cm[("2ams", c, g)] = S.inc("GE", ii, 1)

        @blk.vector
        def _(v, c=c, g=g, i2=i2):
            S.w(v, "GE", cm[("2ams", c, g)])
            last = None
            for h in range(4):
                c4 = (g * NCH + c) * 4
                last = v.scalar_tensor_tensor(
                    at_sb[i2][:, h * 64:(h + 1) * 64], i64[:, :],
                    u_all[:, c4 + h:c4 + h + 1],
                    at_sb[i2][:, h * 64:(h + 1) * 64],
                    ALUm, ALUa)
            cm[("2aad", c, g)] = S.inc("VE", last, 1)

        @blk.tensor
        def _(te, c=c, g=g, i2=i2, io=io, co=co, w=w):
            S.w(te, "VE", cm[("2aad", c, g)])
            S.wt(te, cm[("2avl", c)])
            if c * NG + g >= 2:
                pi = c * NG + g - 2
                S.w(te, "VE", cm.get(("2aoe", pi // NG, pi % NG)))
            last = None
            for pp in range(2):
                te.matmul(ps_o[2 * io + pp][:, 0:32], sublk[2 * pp][:, :],
                          rtw[w % 2][:, co:co + 32],
                          start=True, stop=False, skip_group_check=True)
                te.matmul(ps_o[2 * io + pp][:, 32:64],
                          sublk[2 * pp + 1][:, :],
                          rtw[w % 2][:, co + 32:co + 64],
                          start=False, stop=False, skip_group_check=True)
                for h2 in range(2):
                    h = pp * 2 + h2
                    last = te.matmul(
                        ps_o[2 * io + pp][64 * h2:64 * h2 + 64, 0:64],
                        vw2[c % 2][:, (g * 4 + h) * 64:(g * 4 + h) * 64 + 64],
                        at_sb[i2][:, h * 64:(h + 1) * 64],
                        start=False, stop=True, skip_group_check=True,
                        tile_position=(0, 64 * h2))
            cm[("2aom", c, g)] = S.inc("TE", last, 1)

        @blk.vector
        def _(v, c=c, g=g, io=io):
            S.w(v, "TE", cm[("2aom", c, g)])
            dvA, dvB = 2 * g, 2 * g + 1
            v.tensor_copy(
                o_sbuf[:, dvA * NT + c * 64:dvA * NT + c * 64 + 64],
                ps_o[2 * io][:, :])
            ii = v.tensor_copy(
                o_sbuf[:, dvB * NT + c * 64:dvB * NT + c * 64 + 64],
                ps_o[2 * io + 1][:, :])
            cm[("2aoe", c, g)] = S.inc("VE", ii, 1)


def emit_block_2B(nc, blk, S, cm, b, L, b2, snap2A):
    o_sbuf = L["o_sbuf"]
    indf, bcp, indb = L["indf"], L["bcp"], L["indb"]
    xlgw, wgw, wow = L["xlgw"], L["wgw"], L["wow"]
    sgsb, ogsb = L["sgsb"], L["ogsb"]
    sqsb, stt2, rst2, gtt = L["sqsb"], L["stt2"], L["rst2"], L["gtt"]
    xrd, frd, ost = L["xrd"], L["frd"], L["ost"]
    gnwT, gnbT = L["gnwT"], L["gnbT"]
    eps2 = L["eps_sb"]
    xlg_s, wgb_s, wob_s, ffn_s = L["xlg_s"], L["wgb_s"], L["wob_s"], L["ffn_s"]
    x_h, out_h = L["x_h"], L["out_h"]

    @blk.sync
    def _(q, b=b):
        if b == 0:
            S.wait_snap(q, snap2A)
        if b >= 2:
            S.w(q, "TE", cm.get(("2bsgd", b - 2)))
        for d in range(DP):
            S.inc("Dbx%d" % (b % 2), q.dma_start(
                out=xlgw[b % 2][:, d * 512:(d + 1) * 512],
                in_=xlg_s[d, :, b * BL:(b + 1) * BL]), 16)
        cm[("2bxl", b)] = ("Dbx%d" % (b % 2), S.n("Dbx%d" % (b % 2)))

    with ExitStack() as ses:
        ps_sg = [ses.enter_context(nc.psum_tensor(
            "ps_sg%d_%d" % (b, i), [128, 512], F32)) for i in range(8)]
        for dvh in range(2):
            for d in range(DP):
                i = dvh * DP + d

                @blk.sync
                def _(q, b=b, dvh=dvh, d=d, i=i):
                    if (b, i) != (0, 0):
                        pk = (b, i - 2) if i >= 2 else (b - 1, 2 * DP - 2 + i)
                        S.w(q, "TE", cm.get(("2bgm", pk[0], pk[1])))
                    cm[("2bgl", b, i)] = S.dinc(
                        "Dbw%d" % (i % 2), q.dma_start(
                            out=wgw[i % 2][:, :],
                            in_=wgb_s[d * 128:(d + 1) * 128,
                                      dvh * 1024:(dvh + 1) * 1024]))

                @blk.tensor
                def _(te, b=b, dvh=dvh, d=d, i=i):
                    S.wt(te, cm[("2bgl", b, i)])
                    S.wt(te, cm[("2bxl", b)])
                    if d == 0 and dvh == 0:
                        if b > 0:
                            S.w(te, "SE", cm.get(("2bsil", b - 1, 1)))
                            S.w(te, "VE", cm.get(("2boc", b - 1, 1, 7)))
                    if d == 0 and dvh == 1:
                        S.w(te, "SE", cm.get(("2bsil", b, 0)))
                    ii = None
                    for dv8 in range(8):
                        ii = te.matmul(
                            ps_sg[dv8][:, :],
                            wgw[i % 2][:, dv8 * 128:(dv8 + 1) * 128],
                            xlgw[b % 2][:, d * 512:(d + 1) * 512],
                            start=(d == 0), stop=(d == DP - 1))
                    cm[("2bgm", b, i)] = S.inc("TE", ii, 1)

            @blk.scalar
            def _(sc, b=b, dvh=dvh):
                S.w(sc, "TE", cm[("2bgm", b, dvh * DP + DP - 1)])
                last = None
                for dv8 in range(8):
                    dv = dvh * 8 + dv8
                    last = sc.activation(sgsb[:, dv * 512:(dv + 1) * 512],
                                         ps_sg[dv8][:, :], AF.Silu)
                cm[("2bsil", b, dvh)] = S.inc("SE", last, 1)
        cm[("2bsgd", b)] = cm[("2bgm", b, 2 * DP - 1)]

    with ExitStack() as ges:
        ps_s2 = ges.enter_context(nc.psum_tensor("ps_s2_%d" % b, [32, 512], F32))
        ps_q2 = ges.enter_context(nc.psum_tensor("ps_q2_%d" % b, [32, 512], F32))
        for dv in range(DP):
            @blk.scalar
            def _(sc, b=b, dv=dv):
                if dv == 0:
                    S.wt(sc, cm["c2bld"])
                if dv >= 2:
                    S.w(sc, "TE", cm.get(("2bqm", b, dv - 2)))
                elif b > 0:
                    S.w(sc, "TE", cm.get(("2bqm", b - 1, DP - 2 + dv)))
                ii = sc.activation(
                    sqsb[dv % 2][:, :],
                    o_sbuf[:, dv * NT + b * BL:dv * NT + (b + 1) * BL],
                    AF.Square)
                cm[("2bsq", b, dv)] = S.inc("SE", ii, 1)

            @blk.tensor
            def _(te, b=b, dv=dv):
                if dv == 0:
                    S.wt(te, cm["c2bld"])
                    S.w(te, "SE", cm[("2bsil", b, 1)])
                    if b > 0:
                        S.w(te, "VE", cm.get(("2bst", b - 1)))
                te.matmul(ps_s2[:, :], indb[:, dv * 32:(dv + 1) * 32],
                          o_sbuf[:, dv * NT + b * BL:dv * NT + (b + 1) * BL],
                          start=(dv == 0), stop=(dv == DP - 1))
                S.w(te, "SE", cm[("2bsq", b, dv)])
                ii = te.matmul(ps_q2[:, :], indf[:, dv * 32:(dv + 1) * 32],
                               sqsb[dv % 2][:, :],
                               start=(dv == 0), stop=(dv == DP - 1))
                cm[("2bqm", b, dv)] = S.inc("TE", ii, 1)

        @blk.vector
        def _(v, b=b):
            S.w(v, "TE", cm[("2bqm", b, DP - 1)])
            v.tensor_scalar_mul(stt2[:, 0:512], ps_s2[:, :], -1.0 / 64.0)
            S.fence(v, "VE", v.tensor_scalar_mul(
                stt2[:, 512:1024], ps_q2[:, :], 1.0 / 64.0))
            S.fence(v, "VE", v.tensor_mul(
                stt2[:, 1024:1536], stt2[:, 0:512], stt2[:, 0:512]))
            ii = v.tensor_sub(stt2[:, 1536:2048], stt2[:, 512:1024],
                              stt2[:, 1024:1536])
            cm[("2bst", b)] = S.inc("VE", ii, 1)

        @blk.scalar
        def _(sc, b=b):
            S.w(sc, "VE", cm[("2bst", b)])
            sc.activation(rst2[:, :], stt2[:, 1536:2048], AF.Ln,
                          bias=eps2[0:32, :])
            cm[("2brs", b)] = S.inc("SE", sc.activation(
                rst2[:, :], rst2[:, :], AF.Exp, scale=-0.5), 1)

    with ExitStack() as oes:
        ps_nm2 = oes.enter_context(
            nc.psum_tensor("ps_nm2_%d" % b, [128, 512], F32))
        ps_rs2 = oes.enter_context(
            nc.psum_tensor("ps_rs2_%d" % b, [128, 512], F32))
        for dv in range(DP):
            @blk.tensor
            def _(te, b=b, dv=dv):
                S.w(te, "SE", cm[("2brs", b)])
                if dv > 0:
                    S.w(te, "VE", cm.get(("2bgt", b, dv - 1)))
                elif b > 0:
                    S.w(te, "VE", cm.get(("2bgt", b - 1, DP - 1)))
                te.matmul(ps_nm2[:, :], bcp[:, dv * 128:(dv + 1) * 128],
                          stt2[:, 0:512], start=True, stop=True)
                ii = te.matmul(ps_rs2[:, :], bcp[:, dv * 128:(dv + 1) * 128],
                               rst2[:, :], start=True, stop=True)
                cm[("2bbc", b, dv)] = S.inc("TE", ii, 1)

            @blk.vector
            def _(v, b=b, dv=dv):
                S.w(v, "TE", cm[("2bbc", b, dv)])
                if dv > 0:
                    S.w(v, "GE", cm.get(("2bog", b, dv - 1)))
                elif b > 0:
                    S.w(v, "GE", cm.get(("2bog", b - 1, DP - 1)))
                v.tensor_add(
                    gtt[0][:, :],
                    o_sbuf[:, dv * NT + b * BL:dv * NT + (b + 1) * BL],
                    ps_nm2[:, :])
                ii = v.tensor_mul(gtt[1][:, :], gtt[0][:, :], ps_rs2[:, :])
                cm[("2bgt", b, dv)] = S.inc("VE", ii, 1)

            @blk.vector
            def _(v, b=b, dv=dv):
                S.w(v, "VE", cm[("2bgt", b, dv)])
                ii = v.tensor_scalar(gtt[0][:, :], gtt[1][:, :], gnwT[:, 0:1],
                                     gnbT[:, 0:1], ALU.mult, ALU.add)
                cm[("2bt3", b, dv)] = S.inc("VE", ii, 1)

            @blk.gpsimd
            def _(gp, b=b, dv=dv):
                S.w(gp, "VE", cm[("2bt3", b, dv)])
                S.w(gp, "SE", cm[("2bsil", b, dv // 8)])
                ii = gp.tensor_mul(ogsb[:, dv * 512:(dv + 1) * 512],
                                   gtt[0][:, :],
                                   sgsb[:, dv * 512:(dv + 1) * 512])
                cm[("2bog", b, dv)] = S.inc("GE", ii, 1)

    with ExitStack() as wes:
        ps_wo = [wes.enter_context(nc.psum_tensor(
            "ps_wo%d_%d" % (b, i), [128, 512], F32)) for i in range(8)]
        for dh in range(2):
            for dv in range(DP):
                i = dh * DP + dv

                @blk.sync
                def _(q, b=b, dh=dh, dv=dv, i=i):
                    if (b, i) != (0, 0):
                        pk = (b, i - 2) if i >= 2 else (b - 1, 2 * DP - 2 + i)
                        S.w(q, "TE", cm.get(("2bwm", pk[0], pk[1])))
                    cm[("2bwl", b, i)] = S.dinc(
                        "Dbo%d" % (i % 2), q.dma_start(
                            out=wow[i % 2][:, :],
                            in_=wob_s[dv * 128:(dv + 1) * 128,
                                      dh * 1024:(dh + 1) * 1024]))

                @blk.tensor
                def _(te, b=b, dh=dh, dv=dv, i=i):
                    S.wt(te, cm[("2bwl", b, i)])
                    S.w(te, "GE", cm[("2bog", b, dv)])
                    if dv == 0:
                        if dh == 0:
                            S.w(te, "VE", cm[("2bgt", b, DP - 1)])
                        if dh == 0 and b > 0:
                            S.w(te, "VE", cm.get(("2boc", b - 1, 1, 7)))
                        if dh == 1:
                            S.w(te, "VE", cm.get(("2boc", b, 0, 7)))
                    ii = None
                    for d8 in range(8):
                        ii = te.matmul(
                            ps_wo[d8][:, :],
                            wow[i % 2][:, d8 * 128:(d8 + 1) * 128],
                            ogsb[:, dv * 512:(dv + 1) * 512],
                            start=(dv == 0), stop=(dv == DP - 1))
                    cm[("2bwm", b, i)] = S.inc("TE", ii, 1)

            for d8 in range(8):
                dglob = dh * 8 + d8

                @blk.sync
                def _(q, b=b, dh=dh, d8=d8, dglob=dglob):
                    j = (b * 2 + dh) * 8 + d8
                    if j >= 2:
                        S.w(q, "VE", cm.get(("2bocj", j - 2)))
                    S.inc("Dxf%d" % (j % 2), q.dma_start(
                        out=xrd[j % 2][:, :],
                        in_=x_h[dglob * 128:(dglob + 1) * 128,
                                b * BL:(b + 1) * BL]), 16)
                    cm[("2bxf", j)] = S.dinc(
                        "Dxf%d" % (j % 2), q.dma_start(
                            out=frd[j % 2][:, :],
                            in_=ffn_s[dglob * 128:(dglob + 1) * 128,
                                      b * BL:(b + 1) * BL]))

                @blk.vector
                def _(v, b=b, dh=dh, d8=d8, dglob=dglob):
                    j = (b * 2 + dh) * 8 + d8
                    S.wt(v, cm[("2bxf", j)])
                    S.w(v, "TE", cm[("2bwm", b, dh * DP + DP - 1)])
                    if j >= 2:
                        S.wt(v, cm.get(("2bos", j - 2)))
                    v.tensor_add(ost[j % 2][:, :], xrd[j % 2][:, :],
                                 frd[j % 2][:, :])
                    ii = v.tensor_add(ost[j % 2][:, :], ost[j % 2][:, :],
                                      ps_wo[d8][:, :])
                    cm[("2boc", b, dh, d8)] = S.inc("VE", ii, 1)
                    cm[("2bocj", j)] = cm[("2boc", b, dh, d8)]

                @blk.scalar
                def _(sc, b=b, dh=dh, d8=d8, dglob=dglob):
                    j = (b * 2 + dh) * 8 + d8
                    S.w(sc, "VE", cm[("2boc", b, dh, d8)])
                    cm[("2bos", j)] = S.dinc(
                        "Dou%d" % (j % 2), sc.dma_start(
                            out=out_h[dglob * 128:(dglob + 1) * 128,
                                      b * BL:(b + 1) * BL],
                            in_=ost[j % 2][:, :]))


_cache = {}


def _consts():
    if "c" in _cache:
        return _cache["c"]
    i64 = np.eye(64).astype(BF)
    id128 = np.eye(128).astype(BF)
    bmc = np.zeros((128, 128), np.float32)
    for h in range(4):
        bmc[32 * h:32 * h + 32, 32 * h:32 * h + 32] = 1.0
    pm0c = np.zeros((128, 128), np.float32)
    pm1c = np.zeros((128, 128), np.float32)
    for h2 in range(2):
        pm0c[32 * h2:32 * h2 + 32, 64 * h2:64 * h2 + 64] = 1.0
        pm1c[64 + 32 * h2:64 + 32 * h2 + 32, 64 * h2:64 * h2 + 64] = 1.0
    seg4 = np.zeros((128, 4), np.float32)
    for h in range(4):
        seg4[32 * h:32 * (h + 1), h] = 1.0
    indb = np.zeros((128, 512), np.float32)
    for i in range(16):
        indb[0:64, i * 32 + 2 * i] = 1.0
        indb[64:128, i * 32 + 2 * i + 1] = 1.0
    bcp = np.zeros((32, 2048), np.float32)
    for i in range(16):
        bcp[2 * i, i * 128:i * 128 + 64] = 1.0
        bcp[2 * i + 1, i * 128 + 64:i * 128 + 128] = 1.0
    c = dict(i64_in=i64, id128_in=id128, bm_in=bmc.astype(BF),
             pm0_in=pm0c.astype(BF), pm1_in=pm1c.astype(BF),
             seg4_in=seg4.astype(BF), indb_in=indb.astype(BF),
             indf_in=indb.astype(np.float32), bcp_in=bcp,
             ones1_in=np.ones((128, 1), np.float32),
             onesk1_in=np.ones((1, 128), np.float32),
             ones64_in=np.ones((128, 64), np.float32))
    _cache["c"] = c
    return c


def kernel(**inp):
    f32 = np.float32
    x = np.asarray(inp["L0_attn_hidden_states"], f32)
    con = _consts()

    shared = {
        "wx1": np.ascontiguousarray(np.asarray(inp["L0_attn_Wx1"], f32)),
        "wx2r": np.ascontiguousarray(
            np.asarray(inp["L0_attn_Wx2"], f32).transpose(1, 0, 2)
            .reshape(32, 5 * 2048)),
        "wr": np.ascontiguousarray(np.asarray(inp["L0_attn_Wr"], f32)),
        "ww1": np.ascontiguousarray(np.asarray(inp["L0_attn_Ww1"], f32)),
        "ww2": np.ascontiguousarray(np.asarray(inp["L0_attn_Ww2"], f32)),
        "wk": np.ascontiguousarray(np.asarray(inp["L0_attn_Wk"], f32)),
        "wv": np.ascontiguousarray(np.asarray(inp["L0_attn_Wv"], f32)),
        "wg": np.ascontiguousarray(np.asarray(inp["L0_attn_Wg"], f32)),
        "wo": np.ascontiguousarray(np.asarray(inp["L0_attn_Wo"], f32)),
        "bwT_in": np.ascontiguousarray(
            np.asarray(inp["L0_attn_bw"], f32).reshape(8, 128).T),
        "muT_in": np.ascontiguousarray(
            np.asarray(inp["L0_attn_mu_x"], f32).reshape(16, 128).T),
        "bonusT_in": np.ascontiguousarray(
            np.asarray(inp["L0_attn_bonus"], f32).reshape(1024)
            .reshape(8, 128).T),
        "gnwT_in": np.ascontiguousarray(
            np.tile(np.asarray(inp["L0_attn_gn_w"], f32), 2)[:, None]),
        "gnbT_in": np.ascontiguousarray(
            np.tile(np.asarray(inp["L0_attn_gn_b"], f32), 2)[:, None]),
        "sgT_in": np.ascontiguousarray(
            np.asarray(inp["L0_ffn_s_gate"], f32).reshape(64, 128).T),
        "suT_in": np.ascontiguousarray(
            np.asarray(inp["L0_ffn_s_up"], f32).reshape(64, 128).T),
        "sdT_in": np.ascontiguousarray(
            np.asarray(inp["L0_ffn_s_down"], f32).reshape(16, 128).T),
        "wgate8": np.ascontiguousarray(
            np.asarray(inp["L0_ffn_w_gate"]).reshape(16, 128, 64, 128)
            .transpose(2, 1, 0, 3).reshape(64, 128, 16 * 128)),
        "wup8": np.ascontiguousarray(
            np.asarray(inp["L0_ffn_w_up"]).reshape(16, 128, 64, 128)
            .transpose(2, 1, 0, 3).reshape(64, 128, 16 * 128)),
        "wdn8": np.ascontiguousarray(
            np.asarray(inp["L0_ffn_w_down"]).reshape(64, 128, 16, 128)
            .transpose(2, 1, 0, 3).reshape(16, 128, 64 * 128)),
    }
    shared.update(con)

    in_maps = []
    for core in range(8):
        b, h = core // 2, core % 2
        xfm = np.ascontiguousarray(x[b, h * NT:(h + 1) * NT, :].T)
        if h == 0:
            xprev = np.zeros((128, DP), f32)
            gate = np.zeros((128, 1), f32)
        else:
            xprev = np.ascontiguousarray(
                x[b, NT - 1, :].reshape(16, 128).T)
            gate = np.ones((128, 1), f32)
        m = dict(shared)
        m["x"] = xfm
        m["xprev"] = xprev
        m["gate"] = gate
        in_maps.append(m)

    if "nc" not in _cache:
        _cache["nc"] = build_program()
    nc = _cache["nc"]
    res = run_bass_kernel_spmd(nc, in_maps, core_ids=list(range(8)))
    out = np.zeros((4, 4096, 2048), f32)
    for core in range(8):
        b, h = core // 2, core % 2
        out[b, h * NT:(h + 1) * NT, :] = np.asarray(
            res.results[core]["out"], f32).T
    if DEBUG:
        kernel._last = res
    return out


# version 06
# speedup vs baseline: 1.0443x; 1.0443x over previous
import numpy as np
import ml_dtypes
from contextlib import ExitStack

import concourse.bass as bass
import concourse.mybir as mybir
from concourse.bass_utils import run_bass_kernel_spmd

F32 = mybir.dt.float32
BF16 = mybir.dt.bfloat16
I8 = mybir.dt.int8
AF = mybir.ActivationFunctionType
ALU = mybir.AluOpType

NT = 2048
DP = 16
CH = 64
NCH = NT // CH
NB = 4
BL = 512
NG = 8
EPS = 1e-5
XNW = NT + 1
EW = BL + 1

DEBUG = False
STOP_AFTER = 6
BF = ml_dtypes.bfloat16


class Sems:
    ALL = ("TE", "VE", "SE", "GE", "CCS")

    def __init__(self, nc, es):
        self.nc, self.es, self.m = nc, es, {}

    def _e(self, name):
        if name not in self.m:
            self.m[name] = [self.es.enter_context(self.nc.semaphore(name)), 0]
        return self.m[name]

    def h(self, name):
        return self._e(name)[0]

    def inc(self, name, instr, k=1):
        e = self._e(name)
        instr.then_inc(e[0], k)
        e[1] += k
        return e[1]

    def n(self, name):
        return self._e(name)[1]

    def w(self, eng, name, val):
        if val and val > 0:
            eng.wait_ge(self.h(name), val)

    def dinc(self, slot, instr, k=16):
        return (slot, self.inc(slot, instr, k))

    def wt(self, eng, mark):
        if mark:
            self.w(eng, mark[0], mark[1])

    def fence(self, eng, name, instr):
        self.w(eng, name, self.inc(name, instr, 1))

    def snap(self):
        return {k: e[1] for k, e in self.m.items() if e[1] > 0}

    def wait_snap(self, eng, sn):
        for k, v in sn.items():
            if v > 0:
                eng.wait_ge(self.h(k), v)


class _StopEmit(Exception):
    pass


def build_program():
    nc = bass.Bass("TRN2", target_bir_lowering=False, debug=False,
                   detect_race_conditions=False)

    def din(name, shape, dt):
        return nc.dram_tensor(name, shape, dt, kind="ExternalInput").ap()

    x_h = din("x", [2048, NT], F32)
    xprev_h = din("xprev", [128, DP], F32)
    gate_h = din("gate", [128, 1], F32)
    wx1_h = din("wx1", [2048, 160], F32)
    wx2_h = din("wx2r", [32, 5 * 2048], F32)
    wr_h = din("wr", [2048, 1024], F32)
    ww1_h = din("ww1", [2048, 64], F32)
    ww2_h = din("ww2", [64, 1024], F32)
    wk_h = din("wk", [2048, 1024], F32)
    wv_h = din("wv", [2048, 2048], F32)
    wg_h = din("wg", [2048, 2048], F32)
    wo_h = din("wo", [2048, 2048], F32)
    bw_h = din("bwT_in", [128, 8], F32)
    mu_h = din("muT_in", [128, DP], F32)
    bonus_h = din("bonusT_in", [128, 8], F32)
    gnw_h = din("gnwT_in", [128, 1], F32)
    gnb_h = din("gnbT_in", [128, 1], F32)
    sg_h = din("sgT_in", [128, 64], F32)
    su_h = din("suT_in", [128, 64], F32)
    sd_h = din("sdT_in", [128, DP], F32)
    wgate8_h = din("wgate8", [64, 128, 16 * 128], I8)
    wup8_h = din("wup8", [64, 128, 16 * 128], I8)
    wdn8_h = din("wdn8", [16, 128, 64 * 128], I8)
    id128_h = din("id128_in", [128, 128], BF16)
    i64_h = din("i64_in", [64, 64], BF16)
    bm_h = din("bm_in", [128, 128], BF16)
    pm0_h = din("pm0_in", [128, 128], BF16)
    pm1_h = din("pm1_in", [128, 128], BF16)
    seg4_h = din("seg4_in", [128, 4], BF16)
    indb_h = din("indb_in", [128, 512], BF16)
    indf_h = din("indf_in", [128, 512], F32)
    bcp_h = din("bcp_in", [32, 2048], F32)
    ones1_h = din("ones1_in", [128, 1], F32)
    onesk1_h = din("onesk1_in", [1, 128], F32)
    ones64_h = din("ones64_in", [128, 64], F32)

    out_h = nc.dram_tensor("out", [2048, NT], F32, kind="ExternalOutput").ap()

    def dsc(name, shape, dt, dbg=False):
        kind = "ExternalOutput" if (DEBUG and dbg) else "Internal"
        return nc.dram_tensor(name, shape, dt, kind=kind).ap()

    wrb_s = dsc("wrb", [2048, 1024], BF16)
    wkb_s = dsc("wkb", [2048, 1024], BF16)
    wvb_s = dsc("wvb", [2048, 2048], BF16)
    wgb_s = dsc("wgb", [2048, 2048], BF16)
    wob_s = dsc("wob", [2048, 2048], BF16)
    wx2b_s = dsc("wx2b", [32, 5 * 2048], BF16)
    ffn_s = dsc("ffn", [2048, NT], F32, dbg=True)
    vtok_s = dsc("vtok", [NCH, 64, 2048], BF16, dbg=True)
    rt_s = dsc("rt", [NG, 128, NT], BF16, dbg=True)
    kt_s = dsc("kt", [NG, 128, NT], BF16, dbg=True)
    srec_s = dsc("srec", [NG, 128, NCH * 64], BF16, dbg=True)
    xlg_s = dsc("xlg", [DP, 128, NT], BF16, dbg=True)
    cc_in = dsc("cc_in", [128, 512], F32)
    cc_out = dsc("cc_out", [256, 512], F32)
    if DEBUG:
        dxn_h = nc.dram_tensor("d_xn", [128, DP * XNW], BF16,
                               kind="ExternalOutput").ap()
        de_h = nc.dram_tensor("d_e", [NB, 128, NG * EW], F32,
                              kind="ExternalOutput").ap()
        dr_h = nc.dram_tensor("d_r", [NB, 128, NG * BL], BF16,
                              kind="ExternalOutput").ap()
        do_h = nc.dram_tensor("d_o", [128, DP * NT], BF16,
                              kind="ExternalOutput").ap()
        du_h = nc.dram_tensor("d_u", [64, NG * NCH * 4], F32,
                              kind="ExternalOutput").ap()
        ds0_h = nc.dram_tensor("d_s0", [128, 512], F32,
                               kind="ExternalOutput").ap()
        dcoef_h = nc.dram_tensor("d_coef", [128, NCH * NG], F32,
                                 kind="ExternalOutput").ap()
        dem_h = nc.dram_tensor("d_em", [128, NCH * NG * 3], F32,
                               kind="ExternalOutput").ap()
        dp_h = nc.dram_tensor("d_p", [128, 320], BF16,
                              kind="ExternalOutput").ap()
        dxl_h = nc.dram_tensor("d_xl", [128, 512], BF16,
                               kind="ExternalOutput").ap()

    es = ExitStack()
    try:
      with es:
        def SB(n, s, d):
            return es.enter_context(nc.sbuf_tensor(n, s, d))

        S = Sems(nc, es)
        blk = es.enter_context(nc.Block())
        cm = {}

        id128 = SB("id128", [128, 128], BF16)
        seg4 = SB("seg4", [128, 4], BF16)
        ones1 = SB("ones1", [128, 1], F32)
        onesk1 = SB("onesk1", [1, 128], F32)
        ones64 = SB("ones64", [128, 64], F32)
        muT = SB("muT", [128, DP], F32)
        bwT = SB("bwT", [128, 8], F32)
        bonusT = SB("bonusT", [128, 8], F32)
        gnwT = SB("gnwT", [128, 1], F32)
        gnbT = SB("gnbT", [128, 1], F32)
        sgT = SB("sgT", [128, 64], F32)
        suT = SB("suT", [128, 64], F32)
        sdT = SB("sdT", [128, DP], F32)
        gateT = SB("gateT", [128, 1], F32)
        xprevT = SB("xprevT", [128, DP], F32)
        eps_sb = SB("eps_sb", [128, 1], F32)
        slive = SB("slive", [128, NG * 64], F32)
        grun = SB("grun", [128, NG], F32)
        coef = SB("coef", [128, NCH * NG], F32)
        em0s = SB("em0s", [128, NCH * NG], F32)
        em1s = SB("em1s", [128, NCH * NG], F32)
        fLrs = SB("fLrs", [128, NCH * NG], F32)
        u_all = SB("u_all", [64, NG * NCH * 4], F32)

        @blk.sync
        def _(q):
            for t, h in [(id128, id128_h), (seg4, seg4_h),
                         (ones1, ones1_h), (onesk1, onesk1_h),
                         (ones64, ones64_h), (muT, mu_h), (bwT, bw_h),
                         (bonusT, bonus_h), (gnwT, gnw_h), (gnbT, gnb_h),
                         (sgT, sg_h), (suT, su_h), (sdT, sd_h),
                         (gateT, gate_h), (xprevT, xprev_h)]:
                S.inc("Dco", q.dma_start(out=t[:, :], in_=h[:, :]), 16)
        c_consts = ("Dco", S.n("Dco"))

        @blk.gpsimd
        def _(g):
            g.memset(eps_sb[:, :], EPS)
            g.memset(slive[:, :], 0.0)
            ii = g.memset(grun[:, :], 0.0)
            cm["ginit"] = S.inc("GE", ii, 1)

        with ExitStack() as xes:
            def XSB(n, s, d):
                return xes.enter_context(nc.sbuf_tensor(n, s, d))

            xn = XSB("xn", [128, DP * XNW], BF16)
            wx1_sb = XSB("wx1_sb", [128, DP * 160], BF16)
            ww1_sb = XSB("ww1_sb", [128, DP * 64], BF16)
            ww2_sb = XSB("ww2_sb", [64, 1024], BF16)

            @blk.gpsimd
            def _(g):
                for d in range(DP):
                    S.inc("Dsw", g.dma_start(
                        out=wx1_sb[:, d * 160:(d + 1) * 160],
                        in_=wx1_h[d * 128:(d + 1) * 128, :]), 16)
                    S.inc("Dsw", g.dma_start(
                        out=ww1_sb[:, d * 64:(d + 1) * 64],
                        in_=ww1_h[d * 128:(d + 1) * 128, :]), 16)
                S.inc("Dsw", g.dma_start(out=ww2_sb[:, :],
                                         in_=ww2_h[:, :]), 16)
            cm["smallw"] = ("Dsw", S.n("Dsw"))

            with ExitStack() as pes:
                pcA = pes.enter_context(nc.sbuf_tensor("pcA", [128, 1024], BF16))
                pcB = pes.enter_context(nc.sbuf_tensor("pcB", [128, 1024], BF16))

                pc_jobs = []
                for cp in range(10):
                    pc_jobs.append((wx2_h, wx2b_s, 0, 32,
                                    cp * 1024, 1024))
                for (srch, dsth, w) in [(wr_h, wrb_s, 1024), (wk_h, wkb_s, 1024),
                                        (wv_h, wvb_s, 2048), (wg_h, wgb_s, 2048),
                                        (wo_h, wob_s, 2048)]:
                    for d in range(DP):
                        for cp in range(w // 1024):
                            pc_jobs.append((srch, dsth, d * 128, 128,
                                            cp * 1024, 1024))
                pc_ld = []

                @blk.gpsimd
                def _(g):
                    for i, (srch, dsth, r0, nr, c0, ncs) in enumerate(pc_jobs):
                        buf = pcA if i % 2 == 0 else pcB
                        if i >= 2:
                            g.wait_ge(S.h("Dps%d" % (i % 2)),
                                      16 * ((i - 2) // 2 + 1))
                        pc_ld.append(S.dinc("Dpl%d" % (i % 2), g.dma_start(
                            out=buf[0:nr, 0:ncs],
                            in_=srch[r0:r0 + nr, c0:c0 + ncs])))

                @blk.scalar
                def _(sc):
                    for i, (srch, dsth, r0, nr, c0, ncs) in enumerate(pc_jobs):
                        buf = pcA if i % 2 == 0 else pcB
                        S.wt(sc, pc_ld[i])
                        S.inc("Dps%d" % (i % 2), sc.dma_start(
                            out=dsth[r0:r0 + nr, c0:c0 + ncs],
                            in_=buf[0:nr, 0:ncs]), 16)
                cm["precast"] = [("Dps0", S.n("Dps0")), ("Dps1", S.n("Dps1"))]

                if STOP_AFTER < 1:
                    raise _StopEmit()
                _sc_A = nc.enter_named_scope("phA", False)[0]
                with ExitStack() as aes:
                    xt = [aes.enter_context(nc.sbuf_tensor(
                        "xt%d" % i, [128, 512], F32)) for i in range(4)]
                    sqb = [aes.enter_context(nc.sbuf_tensor(
                        "sqb%d" % i, [128, 512], F32)) for i in range(2)]
                    lnst = aes.enter_context(
                        nc.sbuf_tensor("lnst", [1, 4 * 512], F32))
                    rstd1 = aes.enter_context(
                        nc.sbuf_tensor("rstd1", [1, 512], F32))
                    xtm2 = [aes.enter_context(nc.sbuf_tensor(
                        "xtm%d" % i, [128, 512], F32)) for i in range(2)]
                    pvt = aes.enter_context(
                        nc.sbuf_tensor("pvt", [128, 2 * DP], F32))
                    ps_sum = aes.enter_context(
                        nc.psum_tensor("ps_sum", [1, 512], F32))
                    ps_sq = aes.enter_context(
                        nc.psum_tensor("ps_sq", [1, 512], F32))
                    ps_nm = aes.enter_context(
                        nc.psum_tensor("ps_nm", [128, 512], F32))
                    ps_rs = aes.enter_context(
                        nc.psum_tensor("ps_rs", [128, 512], F32))
                    ps_pv = aes.enter_context(
                        nc.psum_tensor("ps_pv", [1, 2 * DP], F32))
                    ps_pb = aes.enter_context(
                        nc.psum_tensor("ps_pb", [128, 2 * DP], F32))

                    @blk.scalar
                    def _(sc):
                        S.wt(sc, c_consts)
                        cm["pvsq"] = S.inc("SE", sc.activation(
                            pvt[:, 0:DP], xprevT[:, :], AF.Square), 1)

                    @blk.tensor
                    def _(te):
                        S.wt(te, c_consts)
                        for d in range(DP):
                            te.matmul(ps_pv[:, 0:1], ones1[:, :],
                                      xprevT[:, d:d + 1],
                                      start=(d == 0), stop=(d == DP - 1))
                        S.w(te, "SE", cm["pvsq"])
                        ii = None
                        for d in range(DP):
                            ii = te.matmul(ps_pv[:, 1:2], ones1[:, :],
                                           pvt[:, d:d + 1],
                                           start=(d == 0), stop=(d == DP - 1))
                        cm["pv2"] = S.inc("TE", ii, 1)

                    @blk.vector
                    def _(v):
                        S.w(v, "TE", cm["pv2"])
                        v.tensor_scalar_mul(lnst[:, 0:1], ps_pv[:, 0:1],
                                            -1.0 / 2048.0)
                        S.fence(v, "VE", v.tensor_scalar_mul(
                            lnst[:, 512:513], ps_pv[:, 1:2], 1.0 / 2048.0))
                        S.fence(v, "VE", v.tensor_mul(
                            lnst[:, 1024:1025], lnst[:, 0:1], lnst[:, 0:1]))
                        ii = v.tensor_sub(lnst[:, 1536:1537],
                                          lnst[:, 512:513],
                                          lnst[:, 1024:1025])
                        cm["pv3"] = S.inc("VE", ii, 1)

                    @blk.scalar
                    def _(sc):
                        S.w(sc, "VE", cm["pv3"])
                        S.w(sc, "GE", cm["ginit"])
                        sc.activation(rstd1[:, 0:1], lnst[:, 1536:1537],
                                      AF.Ln, bias=eps_sb[0:1, :])
                        cm["pv4"] = S.inc("SE", sc.activation(
                            rstd1[:, 0:1], rstd1[:, 0:1], AF.Exp,
                            scale=-0.5), 1)

                    @blk.tensor
                    def _(te):
                        S.w(te, "SE", cm["pv4"])
                        te.matmul(ps_pb[:, 0:1], onesk1[:, :],
                                  lnst[:, 0:1], start=True, stop=True)
                        ii = te.matmul(ps_pb[:, 1:2], onesk1[:, :],
                                       rstd1[:, 0:1], start=True, stop=True)
                        cm["pv5"] = S.inc("TE", ii, 1)

                    @blk.vector
                    def _(v):
                        S.w(v, "TE", cm["pv5"])
                        S.w(v, "TE", cm["pv2"])
                        S.fence(v, "VE", v.tensor_copy(pvt[:, 0:2],
                                                       ps_pb[:, 0:2]))
                        ii = v.tensor_scalar(pvt[:, DP:2 * DP], xprevT[:, :],
                                             pvt[:, 0:1], pvt[:, 1:2],
                                             ALU.add, ALU.mult)
                        cm["pv6"] = S.inc("VE", ii, 1)

                    @blk.vector
                    def _(v):
                        S.w(v, "VE", cm["pv6"])
                        last = None
                        for d in range(DP):
                            last = v.tensor_copy(xn[:, d * XNW:d * XNW + 1],
                                                 pvt[:, DP + d:DP + d + 1])
                        cm["pv7"] = S.inc("VE", last, 1)

                    ldmark = {}
                    consume = {}
                    li = 0
                    for b in range(NB):
                        for d in range(DP):
                            slot = li % 4

                            @blk.sync
                            def _(q, b=b, d=d, slot=slot, li=li):
                                if li - 4 in consume:
                                    sem, val = consume[li - 4]
                                    S.w(q, sem, val)
                                ldmark[li] = S.dinc(
                                    "Dxt%d" % slot, q.dma_start(
                                        out=xt[slot][:, :],
                                        in_=x_h[d * 128:(d + 1) * 128,
                                                b * BL:(b + 1) * BL]))

                            sqi = b * DP + d

                            @blk.scalar
                            def _(sc, b=b, d=d, slot=slot, li=li, sqi=sqi):
                                S.wt(sc, ldmark[li])
                                if sqi >= 2:
                                    S.w(sc, "TE", cm.get(("asq", sqi - 2)))
                                cm[("ase", sqi)] = S.inc("SE", sc.activation(
                                    sqb[sqi % 2][:, :], xt[slot][:, :],
                                    AF.Square), 1)

                            @blk.tensor
                            def _(te, b=b, d=d, slot=slot, li=li, sqi=sqi):
                                S.wt(te, ldmark[li])
                                if d == 0:
                                    S.w(te, "VE", cm.get(("axn", b - 1)))
                                te.matmul(ps_sum[:, :], ones1[:, :],
                                          xt[slot][:, :],
                                          start=(d == 0), stop=(d == DP - 1))
                                S.w(te, "SE", cm[("ase", sqi)])
                                ii = te.matmul(ps_sq[:, :], ones1[:, :],
                                               sqb[sqi % 2][:, :],
                                               start=(d == 0),
                                               stop=(d == DP - 1))
                                cm[("asq", sqi)] = S.inc("TE", ii, 1)
                                consume[li] = ("TE", cm[("asq", sqi)])
                            li += 1

                        @blk.vector
                        def _(v, b=b):
                            S.w(v, "TE", cm[("asq", b * DP + DP - 1)])
                            v.tensor_scalar_mul(lnst[:, 0:512], ps_sum[:, :],
                                                -1.0 / 2048.0)
                            S.fence(v, "VE", v.tensor_scalar_mul(
                                lnst[:, 512:1024],
                                ps_sq[:, :], 1.0 / 2048.0))
                            S.fence(v, "VE", v.tensor_mul(
                                lnst[:, 1024:1536], lnst[:, 0:512],
                                lnst[:, 0:512]))
                            ii = v.tensor_sub(lnst[:, 1536:2048],
                                              lnst[:, 512:1024],
                                              lnst[:, 1024:1536])
                            cm[("ast", b)] = S.inc("VE", ii, 1)

                        @blk.scalar
                        def _(sc, b=b):
                            S.w(sc, "VE", cm[("ast", b)])
                            sc.activation(rstd1[:, :], lnst[:, 1536:2048],
                                          AF.Ln, bias=eps_sb[0:1, :])
                            cm[("ars", b)] = S.inc("SE", sc.activation(
                                rstd1[:, :], rstd1[:, :], AF.Exp,
                                scale=-0.5), 1)

                        @blk.tensor
                        def _(te, b=b):
                            S.w(te, "VE", cm[("ast", b)])
                            if b > 0:
                                S.w(te, "VE", cm.get(("axn", b - 1)))
                            te.matmul(ps_nm[:, :], onesk1[:, :],
                                      lnst[:, 0:512], start=True, stop=True)
                            S.w(te, "SE", cm[("ars", b)])
                            ii = te.matmul(ps_rs[:, :], onesk1[:, :],
                                           rstd1[:, :], start=True, stop=True)
                            cm[("abc", b)] = S.inc("TE", ii, 1)

                        for d in range(DP):
                            slot = li % 4

                            @blk.sync
                            def _(q, b=b, d=d, slot=slot, li=li):
                                if li - 4 in consume:
                                    sem, val = consume[li - 4]
                                    S.w(q, sem, val)
                                ldmark[li] = S.dinc(
                                    "Dxt%d" % slot, q.dma_start(
                                        out=xt[slot][:, :],
                                        in_=x_h[d * 128:(d + 1) * 128,
                                                b * BL:(b + 1) * BL]))

                            @blk.vector
                            def _(v, b=b, d=d, slot=slot, li=li):
                                S.wt(v, ldmark[li])
                                S.w(v, "TE", cm[("abc", b)])
                                v.tensor_add(xtm2[li % 2][:, :],
                                             xt[slot][:, :], ps_nm[:, :])
                                dst = xn[:, d * XNW + 1 + b * BL:
                                         d * XNW + 1 + (b + 1) * BL]
                                ii = v.tensor_mul(dst, xtm2[li % 2][:, :],
                                                  ps_rs[:, :])
                                cm[("axnt", li)] = S.inc("VE", ii, 1)
                                consume[li] = ("VE", cm[("axnt", li)])
                            li += 1

                        @blk.vector
                        def _(v, b=b):
                            cm[("axn", b)] = cm[("axnt", li - 1)]
                cm["Adone"] = None
                snapA = S.snap()
                if DEBUG:
                    @blk.scalar
                    def _(sc):
                        S.wait_snap(sc, snapA)
                        S.inc("Ddbg", sc.dma_start(out=dxn_h[:, :],
                                                    in_=xn[:, :]), 16)

                nc.leave_named_scope("phA", _sc_A, False)
                if STOP_AFTER < 2:
                    raise _StopEmit()
                _sc_B = nc.enter_named_scope("phB", False)[0]
                with ExitStack() as bes:
                    fuse = bes.enter_context(
                        nc.sbuf_tensor("fuse", [128, 32 * 512], BF16))
                    wgb = [bes.enter_context(nc.sbuf_tensor(
                        "wgb%d" % i, [128, 2048], BF16)) for i in range(2)]
                    wub = [bes.enter_context(nc.sbuf_tensor(
                        "wub%d" % i, [128, 2048], BF16)) for i in range(2)]
                    wdb = [bes.enter_context(nc.sbuf_tensor(
                        "wdb%d" % i, [128, 4096], BF16)) for i in range(2)]
                    silu_t = [bes.enter_context(nc.sbuf_tensor(
                        "silu%d" % i, [128, 512], BF16)) for i in range(2)]
                    facc = bes.enter_context(
                        nc.sbuf_tensor("facc", [128, DP * 512], F32))
                    ffn_t = [bes.enter_context(nc.sbuf_tensor(
                        "ffnt%d" % i, [128, 512], F32)) for i in range(2)]
                    ps_g = [bes.enter_context(nc.psum_tensor(
                        "ps_g%d" % i, [128, 512], F32)) for i in range(2)]
                    ps_u = [bes.enter_context(nc.psum_tensor(
                        "ps_u%d" % i, [128, 512], F32)) for i in range(2)]
                    ps_dn = [bes.enter_context(nc.psum_tensor(
                        "ps_dn%d" % i, [128, 512], F32)) for i in range(2)]

                    for b in range(NB):
                        xoff = 1 + b * BL
                        for hf in range(2):
                            for mi in range(32):
                                m = hf * 32 + mi
                                i = (b * 2 + hf) * 32 + mi

                                @blk.gpsimd
                                def _(g, b=b, m=m, mi=mi, i=i):
                                    if i == 0:
                                        S.wait_snap(g, snapA)
                                    if i >= 2:
                                        S.w(g, "TE", cm.get(("bum", i - 2)))
                                    S.inc("Dbg%d" % (i % 2), g.dma_start(
                                        out=wgb[i % 2][:, :],
                                        in_=wgate8_h[m, :, :]), 16)
                                    cm[("bwl", i)] = S.dinc(
                                        "Dbg%d" % (i % 2), g.dma_start(
                                            out=wub[i % 2][:, :],
                                            in_=wup8_h[m, :, :]))

                                @blk.tensor
                                def _(te, b=b, m=m, mi=mi, i=i, xoff=xoff):
                                    S.wt(te, cm[("bwl", i)])
                                    if i == 0:
                                        S.wait_snap(te, snapA)
                                    if i >= 2:
                                        S.w(te, "SE", cm.get(("bsl", i - 2)))
                                    ii = None
                                    for d in range(DP):
                                        ii = te.matmul(
                                            ps_g[i % 2][:, :],
                                            wgb[i % 2][:, d * 128:(d + 1) * 128],
                                            xn[:, d * XNW + xoff:
                                               d * XNW + xoff + BL],
                                            start=(d == 0), stop=(d == DP - 1))
                                    cm[("bgm", i)] = S.inc("TE", ii, 1)
                                    if i >= 2:
                                        S.w(te, "VE", cm.get(("bfu", i - 2)))
                                    for d in range(DP):
                                        ii = te.matmul(
                                            ps_u[i % 2][:, :],
                                            wub[i % 2][:, d * 128:(d + 1) * 128],
                                            xn[:, d * XNW + xoff:
                                               d * XNW + xoff + BL],
                                            start=(d == 0), stop=(d == DP - 1))
                                    cm[("bum", i)] = S.inc("TE", ii, 1)

                                @blk.scalar
                                def _(sc, m=m, i=i):
                                    S.w(sc, "TE", cm[("bgm", i)])
                                    if i >= 2:
                                        S.w(sc, "VE", cm.get(("bfu", i - 2)))
                                    cm[("bsl", i)] = S.inc(
                                        "SE", sc.activation(
                                            silu_t[i % 2][:, :],
                                            ps_g[i % 2][:, :], AF.Silu,
                                            scale=sgT[:, m:m + 1]), 1)

                                @blk.vector
                                def _(v, m=m, mi=mi, i=i):
                                    S.w(v, "TE", cm[("bum", i)])
                                    S.w(v, "SE", cm[("bsl", i)])
                                    if i >= 1:
                                        S.w(v, "TE", cm.get(("bdm", (i // 32) - 1, DP - 1)))
                                    cm[("bfu", i)] = S.inc(
                                        "VE", v.scalar_tensor_tensor(
                                            fuse[:, mi * 512:(mi + 1) * 512],
                                            ps_u[i % 2][:, :],
                                            suT[:, m:m + 1],
                                            silu_t[i % 2][:, :],
                                            ALU.mult, ALU.mult), 1)

                            ihf = b * 2 + hf
                            for o in range(DP):
                                @blk.gpsimd
                                def _(g, hf=hf, o=o, ihf=ihf):
                                    j = ihf * DP + o
                                    if j >= 2:
                                        S.w(g, "TE", cm.get(("bdm2", j - 2)))
                                    cm[("bdl", j)] = S.dinc(
                                        "Dbd%d" % (j % 2), g.dma_start(
                                            out=wdb[j % 2][:, :],
                                            in_=wdn8_h[o, :, hf * 4096:
                                                       (hf + 1) * 4096]))

                                @blk.tensor
                                def _(te, b=b, hf=hf, o=o, ihf=ihf):
                                    j = ihf * DP + o
                                    S.wt(te, cm[("bdl", j)])
                                    if o == 0:
                                        S.w(te, "VE", cm[("bfu", ihf * 32 + 31)])
                                    if j >= 2:
                                        S.w(te, "VE", cm.get(("bfa", j - 2)))
                                    ii = None
                                    for mi in range(32):
                                        ii = te.matmul(
                                            ps_dn[j % 2][:, :],
                                            wdb[j % 2][:, mi * 128:(mi + 1) * 128],
                                            fuse[:, mi * 512:(mi + 1) * 512],
                                            start=(mi == 0), stop=(mi == 31))
                                    cm[("bdm", ihf, o)] = S.inc("TE", ii, 1)
                                    cm[("bdm2", j)] = cm[("bdm", ihf, o)]

                                @blk.vector
                                def _(v, b=b, hf=hf, o=o, ihf=ihf):
                                    j = ihf * DP + o
                                    S.w(v, "TE", cm[("bdm", ihf, o)])
                                    if hf == 0:
                                        ii = v.tensor_copy(
                                            facc[:, o * 512:(o + 1) * 512],
                                            ps_dn[j % 2][:, :])
                                    else:
                                        ii = v.tensor_add(
                                            facc[:, o * 512:(o + 1) * 512],
                                            facc[:, o * 512:(o + 1) * 512],
                                            ps_dn[j % 2][:, :])
                                    cm[("bfa", j)] = S.inc("VE", ii, 1)

                        for o in range(DP):
                            @blk.vector
                            def _(v, b=b, o=o):
                                S.w(v, "VE", cm[("bfa", (b * 2 + 1) * DP + o)])
                                jo = b * DP + o
                                if jo >= 2:
                                    S.wt(v, cm.get(("bst", jo - 2)))
                                cm[("bft", jo)] = S.inc(
                                    "VE", v.tensor_scalar_mul(
                                        ffn_t[jo % 2][:, :],
                                        facc[:, o * 512:(o + 1) * 512],
                                        sdT[:, o:o + 1]), 1)

                            @blk.scalar
                            def _(sc, b=b, o=o):
                                jo = b * DP + o
                                S.w(sc, "VE", cm[("bft", jo)])
                                cm[("bst", jo)] = S.dinc(
                                    "Dfs%d" % (jo % 2), sc.dma_start(
                                        out=ffn_s[o * 128:(o + 1) * 128,
                                                  b * BL:(b + 1) * BL],
                                        in_=ffn_t[jo % 2][:, :]))
                snapB = S.snap()


            nc.leave_named_scope("phB", _sc_B, False)
            if STOP_AFTER < 3:
                raise _StopEmit()
            with ExitStack() as ces:
                def CSB(n, s, d):
                    return ces.enter_context(nc.sbuf_tensor(n, s, d))

                def CPS(n, s, d=F32):
                    return ces.enter_context(nc.psum_tensor(n, s, d))

                e_sb = CSB("e_sb", [128, NG * EW], F32)
                zt1 = CSB("zt1", [128, 512], BF16)
                zt2 = CSB("zt2", [32, 512], BF16)
                xl = [CSB("xlA", [128, DP * 512], BF16),
                      CSB("xlB", [128, DP * 512], BF16)]
                xtm = [CSB("xtmA", [128, 512], F32),
                       CSB("xtmB", [128, 512], F32)]
                t2b = [CSB("t2A", [128, 512], F32)]
                r_sb = CSB("r_sb", [128, NG * 512], BF16)
                k_sb = CSB("k_sb", [128, NG * 512], BF16)
                tw_sb = CSB("tw_sb", [64, 512], BF16)
                wbuf = [CSB("wbA", [128, 512], BF16),
                        CSB("wbB", [128, 512], BF16)]
                wx2w = [CSB("wx2wA", [128, 2048], BF16)]
                vevst = [CSB("vevA", [128, 512], BF16),
                         CSB("vevB", [128, 512], BF16)]
                vwin = [CSB("vwinA", [64, 2048], BF16),
                        CSB("vwinB", [64, 2048], BF16)]
                rtst = CSB("rtst", [128, NG * 512], BF16)
                ktst = CSB("ktst", [128, NG * 512], BF16)
                srsta = CSB("srsta", [128, NG * 512], BF16)
                xcum = CSB("xcum", [128, NG * 64], F32)
                s2b = CSB("s2b", [128, NG], F32)
                u0b = CSB("u0b", [128, NG], F32)
                m0b = CSB("m0b", [128, NG], F32)
                m1b = CSB("m1b", [128, NG], F32)
                nm0b = CSB("nm0b", [128, NG], F32)
                nm1b = CSB("nm1b", [128, NG], F32)
                flab = CSB("flab", [128, NG], F32)
                ns2b = CSB("ns2b", [128, NG], F32)
                lamb = CSB("lamb", [128, NG], F32)
                cexpb = CSB("cexpb", [128, NG], F32)
                ninb = CSB("ninb", [128, NG], F32)
                mhf = CSB("mhf", [128, 512], F32)
                aer = CSB("aer", [128, 512], F32)
                aek = CSB("aek", [128, 512], F32)
                kfb = CSB("kfb", [128, 16], F32)
                khT_a = CSB("khT_a", [64, 512], BF16)

                ps_m = [CPS("ps_m%d" % i, [128, 512]) for i in range(2)]
                ps_p = [CPS("ps_p%d" % i, [128, 512]) for i in range(4)]
                psTa = CPS("psTa", [128, 512])
                ps_tra = CPS("ps_tra", [64, 512], BF16)

                @blk.gpsimd
                def _(g):
                    S.wait_snap(g, snapB)
                    last = None
                    for gg in range(NG):
                        last = g.memset(e_sb[:, gg * EW:gg * EW + 1], 0.0)
                    cm["einit"] = S.inc("GE", last, 1)

                for p in range(NB):
                    _sc_C = nc.enter_named_scope("phC%d" % p, False)[0]
                    emit_pass_C(nc, blk, S, cm, p, locals())
                    nc.leave_named_scope("phC%d" % p, _sc_C, False)

                snapC = S.snap()

                if STOP_AFTER < 4:
                    raise _StopEmit()
                @blk.scalar
                def _(sc):
                    S.wait_snap(sc, snapC)
                    S.inc("Dcc", sc.dma_start(out=cc_in[:, :],
                                               in_=slive[:, :]), 16)
                cm["ccst"] = ("Dcc", S.n("Dcc"))

                @blk.gpsimd
                def _(g):
                    S.wt(g, cm["ccst"])
                    ii = g.collective_compute(
                        "AllGather", ALU.bypass,
                        replica_groups=[[0, 1], [2, 3], [4, 5], [6, 7]],
                        ins=[cc_in[:, :].opt()],
                        outs=[cc_out[:, :].opt()])
                    S.inc("CCS", ii, 1)


        if STOP_AFTER < 5:
            raise _StopEmit()
        _sc_2A = nc.enter_named_scope("ph2A", False)[0]
        o_sbuf = SB("o_sbuf", [128, DP * NT], BF16)
        snapG = S.snap()
        with ExitStack() as a2:
            def A2SB(n, s, d):
                return a2.enter_context(nc.sbuf_tensor(n, s, d))

            rtw = [A2SB("rtw%d" % i, [128, NG * 512], BF16) for i in range(2)]
            ktw = [A2SB("ktw%d" % i, [128, NG * 512], BF16) for i in range(2)]
            srw = [A2SB("srw%d" % i, [128, NG * 512], BF16) for i in range(2)]
            vw2 = [A2SB("vw2%d" % i, [64, 2048], BF16) for i in range(2)]
            gtt_su = A2SB("gtt_su", [128, 64], F32)
            sublk = [A2SB("sublk%d" % i, [128, 128], BF16) for i in range(4)]
            rblk = [A2SB("rblk%d" % i, [128, 128], BF16) for i in range(3)]
            rL_sb = A2SB("rL_sb", [128, 32], BF16)
            kL_sb = A2SB("kL_sb", [128, 32], BF16)
            at_sb = [A2SB("at%d" % i, [64, 256], BF16) for i in range(2)]
            i64 = A2SB("i64", [64, 64], BF16)
            bm = A2SB("bm", [128, 128], BF16)
            pm0 = A2SB("pm0", [128, 128], BF16)
            pm1 = A2SB("pm1", [128, 128], BF16)
            s0 = A2SB("s0", [128, 512], F32)

            @blk.sync
            def _(q):
                S.wait_snap(q, snapG)
                S.inc("Da2", q.dma_start(out=i64[:, :], in_=i64_h[:, :]), 16)
                S.inc("Da2", q.dma_start(out=bm[:, :], in_=bm_h[:, :]), 16)
                S.inc("Da2", q.dma_start(out=pm0[:, :], in_=pm0_h[:, :]), 16)
                S.inc("Da2", q.dma_start(out=pm1[:, :], in_=pm1_h[:, :]), 16)
                S.w(q, "CCS", S.n("CCS"))
                S.inc("Da2", q.dma_start(out=s0[:, :],
                                         in_=cc_out[0:128, :]), 16)
            cm["s0ld"] = ("Da2", S.n("Da2"))
            if DEBUG:
                @blk.scalar
                def _(sc):
                    S.wt(sc, cm["s0ld"])
                    S.inc("Ddbg", sc.dma_start(out=ds0_h[:, :],
                                                in_=s0[:, :]), 16)
                    S.inc("Ddbg", sc.dma_start(out=dcoef_h[:, :],
                                               in_=coef[:, :]), 16)
                    S.inc("Ddbg", sc.dma_start(out=du_h[:, :],
                                               in_=u_all[:, :]), 16)
                    NN = NCH * NG
                    S.inc("Ddbg", sc.dma_start(out=dem_h[:, 0:NN],
                                               in_=em0s[:, :]), 16)
                    S.inc("Ddbg", sc.dma_start(out=dem_h[:, NN:2 * NN],
                                               in_=em1s[:, :]), 16)
                    S.inc("Ddbg", sc.dma_start(out=dem_h[:, 2 * NN:3 * NN],
                                               in_=fLrs[:, :]), 16)
            ps_att = [a2.enter_context(nc.psum_tensor(
                "ps_at%d" % i, [64, 384], F32)) for i in range(2)]
            ps_o = [a2.enter_context(nc.psum_tensor(
                "ps_o%d" % i, [128, 64], F32)) for i in range(4)]

            @blk.vector
            def _(v):
                S.wait_snap(v, snapG)
                v.memset(ps_att[0][:, :], 0.0)
                v.memset(ps_att[1][:, :], 0.0)
                v.memset(at_sb[0][:, :], 0.0)
                cm["2amz"] = S.inc("VE", v.memset(at_sb[1][:, :], 0.0), 1)

            for c in range(NCH):
                emit_chunk_2A(nc, blk, S, cm, c, locals(), snapG)
        snap2A = S.snap()
        if DEBUG:
            @blk.scalar
            def _(sc):
                S.wait_snap(sc, snap2A)
                S.inc("Ddbg", sc.dma_start(out=do_h[:, :],
                                            in_=o_sbuf[:, :]), 16)

        nc.leave_named_scope("ph2A", _sc_2A, False)
        if STOP_AFTER < 6:
            raise _StopEmit()
        _sc_2B = nc.enter_named_scope("ph2B", False)[0]
        with ExitStack() as b2:
            def B2SB(n, s, d):
                return b2.enter_context(nc.sbuf_tensor(n, s, d))

            indf = B2SB("indf", [128, 512], F32)
            indb = B2SB("indb", [128, 512], BF16)
            bcp = B2SB("bcp", [32, 2048], F32)
            xlgw = [B2SB("xlgw%d" % i, [128, DP * 512], BF16)
                    for i in range(2)]
            wgw = [B2SB("wgw%d" % i, [128, 1024], BF16) for i in range(2)]
            wow = [B2SB("wow%d" % i, [128, 1024], BF16) for i in range(2)]
            sgsb = B2SB("sgsb", [128, DP * 512], BF16)
            ogsb = B2SB("ogsb", [128, DP * 512], BF16)
            sqsb = [B2SB("sq2%d" % i, [128, 512], F32) for i in range(2)]
            stt2 = B2SB("stt2", [32, 4 * 512], F32)
            rst2 = B2SB("rst2", [32, 512], F32)
            gtt = [B2SB("gtt%d" % i, [128, 512], F32) for i in range(2)]
            xrd = [B2SB("xrd%d" % i, [128, 512], F32) for i in range(2)]
            frd = [B2SB("frd%d" % i, [128, 512], F32) for i in range(2)]
            ost = [B2SB("ost%d" % i, [128, 512], F32) for i in range(2)]

            @blk.sync
            def _(q):
                S.wait_snap(q, snap2A)
                S.inc("Db2", q.dma_start(out=indf[:, :],
                                          in_=indf_h[:, :]), 16)
                S.inc("Db2", q.dma_start(out=indb[:, :],
                                          in_=indb_h[:, :]), 16)
                S.inc("Db2", q.dma_start(out=bcp[:, :], in_=bcp_h[:, :]), 16)
            cm["c2bld"] = ("Db2", S.n("Db2"))

            for b in range(NB):
                emit_block_2B(nc, blk, S, cm, b, locals(), b2, snap2A)

        nc.leave_named_scope("ph2B", _sc_2B, False)
        endsnap = S.snap()

        @blk.sync
        def _(q):
            S.wait_snap(q, endsnap)
    except _StopEmit:
        pass
    return nc


def emit_pass_C(nc, blk, S, cm, p, L):
    xn = L["xn"]
    e_sb = L["e_sb"]
    xl, xtm, t2b = L["xl"], L["xtm"], L["t2b"]
    zt1, zt2 = L["zt1"], L["zt2"]
    r_sb, k_sb, tw_sb = L["r_sb"], L["k_sb"], L["tw_sb"]
    wbuf, wx2w = L["wbuf"], L["wx2w"]
    vevst, vwin = L["vevst"], L["vwin"]
    rtst, ktst, srsta = L["rtst"], L["ktst"], L["srsta"]
    ps_m, ps_p = L["ps_m"], L["ps_p"]
    psTa, ps_tra = L["psTa"], L["ps_tra"]
    ps_ua = psTa[0:64, 0:32]
    muT = L["muT"]
    bwT = L["bwT"]
    bonusT = L["bonusT"]
    wx1_sb, ww1_sb, ww2_sb = L["wx1_sb"], L["ww1_sb"], L["ww2_sb"]
    ones64 = L["ones64"]
    gateT = L["gateT"]
    grun, coef, u_all = L["grun"], L["coef"], L["u_all"]
    em0s, em1s, fLrs = L["em0s"], L["em1s"], L["fLrs"]
    slive = L["slive"]
    seg4, id128 = L["seg4"], L["id128"]
    xcum, s2b, ns2b = L["xcum"], L["s2b"], L["ns2b"]
    u0b, m0b, m1b = L["u0b"], L["m0b"], L["m1b"]
    nm0b, nm1b, flab = L["nm0b"], L["nm1b"], L["flab"]
    lamb, cexpb, ninb = L["lamb"], L["cexpb"], L["ninb"]
    mhf, aer, aek, kfb = L["mhf"], L["aer"], L["aek"], L["kfb"]
    khT_a = L["khT_a"]
    khata = aek.bitcast(BF16)[:, 0:512]
    bka = aer.bitcast(BF16)[:, 512:1024]
    p_a = aer.bitcast(BF16)[:, 0:512]
    wrb_s, wkb_s, wvb_s, wx2b_s = (L["wrb_s"], L["wkb_s"], L["wvb_s"],
                                   L["wx2b_s"])
    vtok_s, rt_s, kt_s, srec_s, xlg_s = (L["vtok_s"], L["rt_s"], L["kt_s"],
                                         L["srec_s"], L["xlg_s"])
    snapB = L["snapB"]
    DEBUGl = L["DEBUG"] if "DEBUG" in L else False

    xoff = 1 + p * BL

    @blk.vector
    def _(v):
        if p == 0:
            S.wait_snap(v, snapB)
            for mk in cm["precast"]:
                S.wt(v, mk)
            S.w(v, "GE", cm["einit"])
        else:
            for g in range(NG):
                v.tensor_copy(e_sb[:, g * EW:g * EW + 1],
                              e_sb[:, g * EW + BL:g * EW + BL + 1])
        cm[("bnd", p)] = S.inc("VE", v.engine_nop(), 1)

    @blk.sync
    def _(q):
        if p == 0:
            S.wait_snap(q, snapB)
            for mk in cm["precast"]:
                S.wt(q, mk)
        for proj in range(5):
            if (p, proj) != (0, 0):
                pp_, pj_ = (p, proj - 1) if proj >= 1 else (p - 1, 4)
                S.w(q, "TE", cm.get(("mml", pp_, pj_)))
            pr = 32 * (proj % 4)
            cm[("wx2l", p, proj)] = S.dinc("Dwx2", q.dma_start(
                out=wx2w[0][pr:pr + 32, :],
                in_=wx2b_s[:, proj * 2048:(proj + 1) * 2048]))

    for d in range(DP):
        @blk.vector
        def _(v, d=d):
            if d == 0:
                S.w(v, "VE", cm[("bnd", p)])
                if p > 0:
                    S.w(v, "TE", cm.get(("pjc", p - 1, 3)))
            cur = xn[:, d * XNW + xoff:d * XNW + xoff + BL]
            prv = xn[:, d * XNW + xoff - 1:d * XNW + xoff - 1 + BL]
            v.tensor_sub(xtm[d % 2][:, :], prv, cur)
            ii = v.scalar_tensor_tensor(
                xl[0][:, d * 512:(d + 1) * 512], xtm[d % 2][:, :],
                muT[:, d:d + 1], cur, ALU.mult, ALU.add)
            cm[("zxl", p, d)] = S.inc("VE", ii, 1)

    for d in range(DP):
        @blk.tensor
        def _(te, d=d):
            if d == 0:
                S.wt(te, cm["smallw"])
                if p == 0:
                    S.wait_snap(te, snapB)
                else:
                    S.w(te, "SE", cm.get(("zev", p - 1)))
                    S.w(te, "VE", cm.get(("vevl", p - 1)))
                    S.w(te, "SE", cm.get(("eev", p - 1, NG - 1)))
            S.w(te, "VE", cm[("zxl", p, d)])
            te.matmul(ps_p[0][:, :], wx1_sb[:, d * 160:d * 160 + 128],
                      xl[0][:, d * 512:(d + 1) * 512],
                      start=(d == 0), stop=(d == DP - 1))
            ii = te.matmul(ps_p[1][0:32, :],
                           wx1_sb[:, d * 160 + 128:d * 160 + 160],
                           xl[0][:, d * 512:(d + 1) * 512],
                           start=(d == 0), stop=(d == DP - 1))
            cm[("zmm", p, d)] = S.inc("TE", ii, 1)

    @blk.scalar
    def _(sc):
        S.w(sc, "TE", cm[("zmm", p, DP - 1)])
        sc.activation(zt1[:, :], ps_p[0][:, :], AF.Tanh)
        cm[("zev", p)] = S.inc("SE", sc.activation(
            zt2[:, :], ps_p[1][0:32, :], AF.Tanh), 1)

    for proj in range(5):
        dst = xl[(proj + 1) % 2]

        @blk.sync
        def _(q, proj=proj):
            if (p, proj) != (0, 0):
                prevk = ("mml", p, proj - 1) if proj >= 1 else \
                    ("mml", p - 1, 4)
                S.w(q, "TE", cm.get(prevk))
            pr = 32 * (proj % 4)
            cm[("wx2l", p, proj)] = S.dinc("Dwx2", q.dma_start(
                out=wx2w[0][pr:pr + 32, :],
                in_=wx2b_s[:, proj * 2048:(proj + 1) * 2048]))

        for d in range(DP):
            @blk.tensor
            def _(te, proj=proj, d=d):
                if d == 0:
                    S.wt(te, cm[("wx2l", p, proj)])
                    S.w(te, "SE", cm[("zev", p)])
                zsrc = (zt1[32 * proj:32 * proj + 32, :] if proj < 4
                        else zt2[:, :])
                tp = (32 * proj, 0) if proj < 4 else (0, 0)
                key = ("t2c", p, proj, d - 2)
                if d < 2:
                    key = ("t2c", p, proj - 1, DP - 2 + d) if proj > 0 else \
                        (("t2c", p - 1, 4, DP - 2 + d) if p > 0 else None)
                if key:
                    S.w(te, "VE", cm.get(key))
                pr = 32 * (proj % 4)
                ii = te.matmul(
                    ps_m[d % 2][:, :],
                    wx2w[0][pr:pr + 32, d * 128:(d + 1) * 128],
                    zsrc, start=True, stop=True, tile_position=tp)
                cm[("mmi", p, proj, d)] = S.inc("TE", ii, 1)
                if d == DP - 1:
                    cm[("mml", p, proj)] = cm[("mmi", p, proj, d)]

            @blk.vector
            def _(v, proj=proj, d=d, dst=dst):
                if d == 0:
                    if proj == 0:
                        if p > 0:
                            S.w(v, "GE", cm.get(("xlg_done", p - 1)))
                    elif proj == 1:
                        S.w(v, "TE", cm.get(("zmm", p, DP - 1)))
                    else:
                        S.w(v, "TE", cm.get(("pjc", p, proj - 2)))
                cur = xn[:, d * XNW + xoff:d * XNW + xoff + BL]
                prv = xn[:, d * XNW + xoff - 1:d * XNW + xoff - 1 + BL]
                v.tensor_sub(xtm[d % 2][:, :], prv, cur)
                pkey = ("xlr", p, proj, d - 1)
                if d < 1:
                    pkey = (("xlr", p, proj - 1, DP - 1) if proj > 0
                            else (("xlr", p - 1, 4, DP - 1) if p > 0
                                  else None))
                if pkey:
                    S.w(v, "GE", cm.get(pkey))
                S.w(v, "TE", cm[("mmi", p, proj, d)])
                ii = v.tensor_mul(t2b[0][:, :], xtm[d % 2][:, :],
                                  ps_m[d % 2][:, :])
                cm[("t2c", p, proj, d)] = S.inc("VE", ii, 1)

            @blk.gpsimd
            def _(gp, proj=proj, d=d, dst=dst):
                S.w(gp, "VE", cm[("t2c", p, proj, d)])
                cur = xn[:, d * XNW + xoff:d * XNW + xoff + BL]
                ii = gp.tensor_add(dst[:, d * 512:(d + 1) * 512],
                                   t2b[0][:, :], cur)
                cm[("xlr", p, proj, d)] = S.inc("GE", ii, 1)

        if proj in (0, 2):
            wsrc = wrb_s if proj == 0 else wkb_s
            osb = r_sb if proj == 0 else k_sb
            for mg in range(2):
                for d in range(DP):
                    i = mg * DP + d

                    @blk.sync
                    def _(q, proj=proj, wsrc=wsrc, mg=mg, d=d, i=i):
                        if i >= 2:
                            S.w(q, "TE", cm.get(("pwm", p, proj, i - 2)))
                        elif proj == 2:
                            S.w(q, "TE", cm.get(("pwm", p, 0, 2 * DP - 2 + i)))
                        elif proj == 0 and p > 0:
                            S.w(q, "TE", cm.get(("vwm", p - 1, 4 * DP - 2 + i)))
                        cm[("pwl", p, proj, i)] = S.dinc(
                            "Dwb%d" % (i % 2), q.dma_start(
                                out=wbuf[i % 2][:, :],
                                in_=wsrc[d * 128:(d + 1) * 128,
                                         mg * 512:(mg + 1) * 512]))

                    @blk.tensor
                    def _(te, proj=proj, mg=mg, d=d, i=i):
                        S.wt(te, cm[("pwl", p, proj, i)])
                        S.w(te, "GE", cm[("xlr", p, proj, d)])
                        if d == 0 and mg == 1:
                            S.w(te, "VE", cm[("pev", p, proj, 0)])
                        if d == 0 and mg == 0:
                            key = ("pev", p, proj - 2, 1) if proj == 2 else \
                                (("vevl", p - 1) if p > 0 else None)
                            if key:
                                S.w(te, "VE", cm.get(key))
                        last = None
                        for m in range(4):
                            last = te.matmul(
                                ps_p[m][:, :],
                                wbuf[i % 2][:, m * 128:(m + 1) * 128],
                                xl[(proj + 1) % 2][:, d * 512:(d + 1) * 512],
                                start=(d == 0), stop=(d == DP - 1))
                        cm[("pwm", p, proj, i)] = S.inc("TE", last, 1)

                @blk.vector
                def _(v, proj=proj, osb=osb, mg=mg):
                    S.w(v, "TE", cm[("pwm", p, proj, mg * DP + DP - 1)])
                    last = None
                    for m in range(4):
                        g = mg * 4 + m
                        last = v.tensor_copy(osb[:, g * 512:(g + 1) * 512],
                                             ps_p[m][:, :])
                    cm[("pev", p, proj, mg)] = S.inc("VE", last, 1)
            cm[("pjc", p, proj)] = cm[("pwm", p, proj, 2 * DP - 1)]
        elif proj == 1:
            for d in range(DP):
                @blk.tensor
                def _(te, d=d):
                    if d == 0:
                        key = ("pev", p, 0, 1)
                        S.w(te, "VE", cm.get(key))
                    S.w(te, "GE", cm[("xlr", p, 1, d)])
                    ii = te.matmul(ps_p[0][0:64, :],
                                   ww1_sb[:, d * 64:(d + 1) * 64],
                                   xl[0][:, d * 512:(d + 1) * 512],
                                   start=(d == 0), stop=(d == DP - 1))
                    cm[("twm", p, d)] = S.inc("TE", ii, 1)

            @blk.scalar
            def _(sc):
                S.w(sc, "TE", cm[("twm", p, DP - 1)])
                S.w(sc, "VE", cm[("bnd", p)])
                cm[("twe", p)] = S.inc("SE", sc.activation(
                    tw_sb[:, :], ps_p[0][0:64, :], AF.Tanh), 1)

            for m in range(NG):
                @blk.tensor
                def _(te, m=m):
                    if m == 0:
                        S.w(te, "SE", cm[("twe", p)])
                    if m >= 2:
                        S.w(te, "SE", cm.get(("eev", p, m - 2)))
                    elif p > 0 and m < 2:
                        S.w(te, "SE", cm.get(("eev", p - 1, NG - 2 + m)))
                    ii = te.matmul(ps_p[1 + m % 2][:, :],
                                   ww2_sb[:, m * 128:(m + 1) * 128],
                                   tw_sb[:, :], start=True, stop=True)
                    cm[("wmm", p, m)] = S.inc("TE", ii, 1)

                @blk.scalar
                def _(sc, m=m):
                    S.w(sc, "TE", cm[("wmm", p, m)])
                    if p > 0 and m == 0:
                        S.w(sc, "VE", cm.get(("scandone", p - 1)))
                    ii = sc.activation(
                        e_sb[:, m * EW + 1:m * EW + 1 + BL],
                        ps_p[1 + m % 2][:, :], AF.Exp, bias=bwT[:, m:m + 1])
                    cm[("eev", p, m)] = S.inc("SE", ii, 1)
            cm[("pjc", p, 1)] = cm[("twm", p, DP - 1)]
        elif proj == 3:
            for dvc in range(4):
                for d in range(DP):
                    i = dvc * DP + d

                    @blk.sync
                    def _(q, dvc=dvc, d=d, i=i):
                        if i >= 2:
                            S.w(q, "TE", cm.get(("vwm", p, i - 2)))
                        else:
                            S.w(q, "TE", cm.get(("pwm", p, 2, 2 * DP - 2 + i)))
                        cm[("vwl", p, i)] = S.dinc(
                            "Dwb%d" % (i % 2), q.dma_start(
                                out=wbuf[i % 2][:, :],
                                in_=wvb_s[d * 128:(d + 1) * 128,
                                          dvc * 512:(dvc + 1) * 512]))

                    @blk.tensor
                    def _(te, dvc=dvc, d=d, i=i):
                        S.wt(te, cm[("vwl", p, i)])
                        S.w(te, "GE", cm[("xlr", p, 3, d)])
                        if d == 0:
                            if dvc == 0:
                                S.w(te, "VE", cm.get(("pev", p, 2, 1)))
                            else:
                                S.w(te, "VE", cm.get(("vev", p, dvc - 1)))
                        last = None
                        for t in range(4):
                            last = te.matmul(
                                ps_p[t][:, :],
                                xl[0][:, d * 512 + t * 128:
                                      d * 512 + (t + 1) * 128],
                                wbuf[i % 2][:, :],
                                start=(d == 0), stop=(d == DP - 1))
                        cm[("vwm", p, i)] = S.inc("TE", last, 1)

                for t in range(4):
                    @blk.vector
                    def _(v, dvc=dvc, t=t):
                        if t == 0:
                            S.w(v, "TE", cm[("vwm", p, dvc * DP + DP - 1)])
                        j = dvc * 4 + t
                        if j >= 2:
                            S.wt(v, cm.get(("vst", p, j - 2)))
                        ii = v.tensor_copy(vevst[j % 2][:, :], ps_p[t][:, :])
                        cm[("vevt", p, j)] = S.inc("VE", ii, 1)

                    @blk.scalar
                    def _(sc, dvc=dvc, t=t):
                        j = dvc * 4 + t
                        S.w(sc, "VE", cm[("vevt", p, j)])
                        c0 = p * 8 + 2 * t
                        S.inc("Dvs%d" % (j % 2), sc.dma_start(
                            out=vtok_s[c0, :, dvc * 512:(dvc + 1) * 512],
                            in_=vevst[j % 2][0:64, :]), 16)
                        ii = sc.dma_start(
                            out=vtok_s[c0 + 1, :, dvc * 512:(dvc + 1) * 512],
                            in_=vevst[j % 2][64:128, :])
                        cm[("vst", p, j)] = S.dinc("Dvs%d" % (j % 2), ii)

                @blk.vector
                def _(v, dvc=dvc):
                    cm[("vev", p, dvc)] = cm[("vevt", p, dvc * 4 + 3)]
            cm[("pjc", p, 3)] = cm[("vwm", p, 4 * DP - 1)]
            cm[("vevl", p)] = cm[("vevt", p, 15)]
        else:
            for d in range(DP):
                @blk.scalar
                def _(sc, d=d):
                    S.w(sc, "GE", cm[("xlr", p, 4, d)])
                    ii = sc.dma_start(
                        out=xlg_s[d, :, p * BL:(p + 1) * BL],
                        in_=xl[1][:, d * 512:(d + 1) * 512])
                    cm[("gst", p, d)] = S.dinc("Dgst", ii)

            @blk.gpsimd
            def _(gp):
                S.wt(gp, cm[("gst", p, DP - 1)])
                cm[("xlg_done", p)] = S.inc("GE", gp.engine_nop(), 1)
            cm[("pjc", p, 4)] = 0

    def b8(t, n=64):
        return t[:, :].rearrange("p (g a) -> p g a", a=1).broadcast_to(
            [128, NG, n])

    def g3(t, w, cl, n=64):
        return t[:, :].rearrange("p (g w) -> p g w", w=w)[
            :, :, cl * 64:cl * 64 + n]

    for cl in range(8):
        cglob = p * 8 + cl

        @blk.sync
        def _(q, cl=cl, cglob=cglob):
            S.wt(q, cm[("vst", p, 12 + cl // 2)])
            if cglob >= 2:
                S.w(q, "TE", cm.get(("tmma", cglob - 2)))
            cm[("vwl2", cglob)] = S.dinc(
                "Dvw%d" % (cl % 2), q.dma_start(
                    out=vwin[cl % 2][:, :], in_=vtok_s[cglob, :, :]))

        @blk.vector
        def _(v, cl=cl, cglob=cglob):
            if cglob > 0:
                S.w(v, "SE", cm.get(("seba", cglob - 1)))
            if cl == 0:
                for g in range(NG):
                    S.w(v, "SE", cm[("eev", p, g)])
                S.wt(v, cm.get(("cst", p - 1)))
                S.w(v, "GE", cm["ginit"])
            S.fence(v, "VE", v.tensor_scalar_mul(
                ninb[:, :], e_sb[:, cl * 64::EW], -1.0))
            last = None
            for g in range(NG):
                last = v.tensor_tensor_scan(
                    xcum[:, g * 64:(g + 1) * 64],
                    e_sb[:, g * EW + cl * 64:g * EW + cl * 64 + 64],
                    ones64[:, :], ninb[:, g:g + 1], ALU.add, ALU.mult)
            S.fence(v, "VE", last)
            v.tensor_add(s2b[:, :], xcum[:, 63::64],
                         e_sb[:, cl * 64 + 64::EW])
            S.fence(v, "VE", v.tensor_scalar_mul(
                m0b[:, :], xcum[:, 32::64], 0.5))
            v.tensor_scalar_mul(nm0b[:, :], m0b[:, :], -1.0)
            v.tensor_scalar_mul(ns2b[:, :], s2b[:, :], -1.0)
            S.fence(v, "VE", v.tensor_scalar_mul(
                ninb[:, :], s2b[:, :], 0.5))
            v.tensor_add(m1b[:, :], m0b[:, :], ninb[:, :])
            S.fence(v, "VE", v.tensor_sub(
                flab[:, :], m0b[:, :], ninb[:, :]))
            v.tensor_scalar_mul(nm1b[:, :], m1b[:, :], -1.0)
            v.tensor_sub(kfb[:, 0:8], m0b[:, :], s2b[:, :])
            v.tensor_sub(kfb[:, 8:16], m1b[:, :], s2b[:, :])
            m3 = mhf[:, :].rearrange("p (g t) -> p g t", t=64)
            v.tensor_copy(m3[:, :, 0:32], b8(m0b, 32))
            S.fence(v, "VE", v.tensor_copy(m3[:, :, 32:64], b8(m1b, 32)))
            xc3 = xcum[:, :].rearrange("p (g t) -> p g t", t=64)
            S.fence(v, "VE", v.tensor_sub(
                aer[:, :].rearrange("p (g t) -> p g t", t=64), m3, xc3))
            ii = v.tensor_sub(
                aek[:, :].rearrange("p (g t) -> p g t", t=64),
                e_sb[:, :].rearrange("p (g w) -> p g w", w=EW)[
                    :, :, cl * 64 + 1:cl * 64 + 65],
                aer[:, :].rearrange("p (g t) -> p g t", t=64))
            cm[("sc1", cglob)] = S.inc("VE", ii, 1)

        @blk.scalar
        def _(sc, cl=cl, cglob=cglob):
            S.w(sc, "VE", cm[("sc1", cglob)])
            if cglob > 0:
                S.w(sc, "VE", cm.get(("supa", cglob - 1)))
                S.w(sc, "TE", cm.get(("umma", cglob - 1)))
                S.w(sc, "TE", cm.get(("trmb", cglob - 1)))
            sc.activation(lamb[:, :], ns2b[:, :], AF.Exp)
            sc.activation(cexpb[:, :], grun[:, :], AF.Exp, scale=-1.0)
            sc.activation(em0s[:, cglob * NG:(cglob + 1) * NG],
                          nm0b[:, :], AF.Exp)
            sc.activation(em1s[:, cglob * NG:(cglob + 1) * NG],
                          nm1b[:, :], AF.Exp)
            sc.activation(fLrs[:, cglob * NG:(cglob + 1) * NG],
                          flab[:, :], AF.Exp)
            sc.activation(kfb[:, :], kfb[:, :], AF.Exp)
            sc.activation(aer[:, :], aer[:, :], AF.Exp)
            ii = sc.activation(aek[:, :], aek[:, :], AF.Exp)
            cm[("seba", cglob)] = S.inc("SE", ii, 1)

        @blk.vector
        def _(v, cl=cl, cglob=cglob):
            S.w(v, "SE", cm[("seba", cglob)])
            v.tensor_scalar_mul(coef[:, cglob * NG:(cglob + 1) * NG],
                                cexpb[:, :], gateT[:, 0:1])
            ii = v.tensor_add(grun[:, :], grun[:, :], s2b[:, :])
            cm[("sc2", cglob)] = S.inc("VE", ii, 1)

        @blk.vector
        def _(v, cl=cl, cglob=cglob):
            S.w(v, "SE", cm[("seba", cglob)])
            if cl == 0 and p > 0:
                S.wt(v, cm.get(("rkst", p - 1)))
            er3 = aer[:, :].rearrange("p (g t) -> p g t", t=64)
            ek3 = aek[:, :].rearrange("p (g t) -> p g t", t=64)
            v.tensor_mul(g3(rtst, 512, cl), g3(r_sb, 512, cl), er3)
            v.tensor_mul(g3(ktst, 512, cl), g3(k_sb, 512, cl), ek3)
            kf3 = mhf[:, :].rearrange("p (g t) -> p g t", t=64)
            v.tensor_copy(kf3[:, :, 0:32], b8(kfb[:, 0:8], 32))
            S.fence(v, "VE", v.tensor_copy(kf3[:, :, 32:64],
                                           b8(kfb[:, 8:16], 32)))
            ii = v.tensor_mul(
                khata[:, :].rearrange("p (g t) -> p g t", t=64),
                g3(ktst, 512, cl), kf3)
            cm[("khwa", cglob)] = S.inc("VE", ii, 1)
            v.tensor_mul(
                bka[:, :].rearrange("p (g t) -> p g t", t=64),
                g3(k_sb, 512, cl), b8(bonusT, 64))
            ii = v.tensor_mul(
                p_a[:, :].rearrange("p (g t) -> p g t", t=64),
                g3(r_sb, 512, cl),
                bka[:, :].rearrange("p (g t) -> p g t", t=64))
            cm[("pwa", cglob)] = S.inc("VE", ii, 1)

        @blk.tensor
        def _(te, cl=cl, cglob=cglob):
            S.w(te, "VE", cm[("pwa", cglob)])
            if cglob > 0:
                S.w(te, "VE", cm.get(("khTa", cglob - 1)))
                S.w(te, "VE", cm.get(("supa", cglob - 1)))
            ii = None
            for g in range(NG):
                ii = te.matmul(ps_ua[:, 4 * g:4 * g + 4],
                               p_a[:, g * 64:(g + 1) * 64], seg4[:, :],
                               start=True, stop=True)
            cm[("umma", cglob)] = S.inc("TE", ii, 1)

        @blk.tensor
        def _(te, cl=cl, cglob=cglob):
            S.w(te, "VE", cm[("khwa", cglob)])
            if cglob > 0:
                S.w(te, "VE", cm.get(("khTb", cglob - 1)))
            ii = None
            for g in range(4):
                ii = te.transpose(ps_tra[:, g * 128:(g + 1) * 128],
                                  khata[:, g * 64:(g + 1) * 64],
                                  id128[:, :])
            cm[("trma", cglob)] = S.inc("TE", ii, 1)

        @blk.vector
        def _(v, cl=cl, cglob=cglob):
            S.w(v, "TE", cm[("trma", cglob)])
            S.w(v, "TE", cm[("umma", cglob)])
            if cglob > 0:
                S.w(v, "TE", cm.get(("tmma", cglob - 1)))
            ua3 = u_all[:, :].rearrange("p (g x) -> p g x", x=NCH * 4)[
                :, :, cglob * 4:cglob * 4 + 4]
            v.tensor_copy(ua3, ps_ua[:, :].rearrange(
                "p (g f) -> p g f", f=4))
            ii = v.tensor_copy(khT_a[:, :], ps_tra[:, :])
            cm[("khTa", cglob)] = S.inc("VE", ii, 1)

        @blk.tensor
        def _(te, cl=cl, cglob=cglob):
            S.w(te, "VE", cm[("khTa", cglob)])
            S.wt(te, cm[("vwl2", cglob)])
            if cglob > 0:
                S.w(te, "VE", cm.get(("supa", cglob - 1)))
            last = None
            for g in range(4):
                for h in range(4):
                    last = te.matmul(
                        psTa[32 * h:32 * h + 32, g * 64:(g + 1) * 64],
                        khT_a[:, g * 128 + 32 * h:g * 128 + 32 * h + 32],
                        vwin[cl % 2][:, (g * 4 + h) * 64:
                                     (g * 4 + h) * 64 + 64],
                        start=True, stop=True,
                        tile_position=(0, 32 * h))
            cm[("tmmha", cglob)] = S.inc("TE", last, 1)
            ii = None
            for g in range(4, NG):
                ii = te.transpose(ps_tra[:, (g - 4) * 128:(g - 3) * 128],
                                  khata[:, g * 64:(g + 1) * 64],
                                  id128[:, :])
            cm[("trmb", cglob)] = S.inc("TE", ii, 1)

        @blk.vector
        def _(v, cl=cl, cglob=cglob):
            S.w(v, "TE", cm[("trmb", cglob)])
            S.w(v, "TE", cm[("tmmha", cglob)])
            ii = v.tensor_copy(khT_a[:, :], ps_tra[:, :])
            cm[("khTb", cglob)] = S.inc("VE", ii, 1)

        @blk.tensor
        def _(te, cl=cl, cglob=cglob):
            S.w(te, "VE", cm[("khTb", cglob)])
            last = None
            for g in range(4, NG):
                for h in range(4):
                    last = te.matmul(
                        psTa[32 * h:32 * h + 32, g * 64:(g + 1) * 64],
                        khT_a[:, (g - 4) * 128 + 32 * h:
                              (g - 4) * 128 + 32 * h + 32],
                        vwin[cl % 2][:, (g * 4 + h) * 64:
                                     (g * 4 + h) * 64 + 64],
                        start=True, stop=True,
                        tile_position=(0, 32 * h))
            cm[("tmma", cglob)] = S.inc("TE", last, 1)

        @blk.vector
        def _(v, cl=cl, cglob=cglob):
            v.tensor_copy(g3(srsta, 512, cl), slive[:, :].rearrange(
                "p (g t) -> p g t", t=64))
            S.fence(v, "VE", v.tensor_mul(
                slive[:, :].rearrange("p (g t) -> p g t", t=64),
                slive[:, :].rearrange("p (g t) -> p g t", t=64),
                b8(lamb, 64)))
            S.w(v, "TE", cm[("tmma", cglob)])
            ii = v.tensor_add(slive[:, :], slive[:, :], psTa[:, :])
            cm[("supa", cglob)] = S.inc("VE", ii, 1)

    @blk.vector
    def _(v):
        cm[("scandone", p)] = cm[("sc2", p * 8 + 7)]

    @blk.scalar
    def _(sc):
        S.w(sc, "VE", cm[("khwa", p * 8 + 7)])
        for g in range(NG):
            S.inc("Drk", sc.dma_start(
                out=rt_s[g, :, p * BL:(p + 1) * BL],
                in_=rtst[:, g * 512:(g + 1) * 512]), 16)
            S.inc("Drk", sc.dma_start(
                out=kt_s[g, :, p * BL:(p + 1) * BL],
                in_=ktst[:, g * 512:(g + 1) * 512]), 16)
        cm[("rkst", p)] = ("Drk", S.n("Drk"))
        S.w(sc, "VE", cm[("supa", p * 8 + 7)])
        for g in range(NG):
            S.inc("Dsr", sc.dma_start(
                out=srec_s[g, :, p * 8 * 64:(p + 1) * 8 * 64],
                in_=srsta[:, g * 512:(g + 1) * 512]), 16)
        cm[("cst", p)] = ("Dsr", S.n("Dsr"))
    if L.get("DEBUG_ON"):
        pass

    if DEBUG:
        @blk.scalar
        def _(sc):
            S.w(sc, "VE", cm[("supa", p * 8 + 7)])
            de_h = L["de_h"]
            dr_h = L["dr_h"]
            S.inc("Ddbg", sc.dma_start(out=de_h[p, :, :],
                                        in_=e_sb[:, :]), 16)
            S.inc("Ddbg", sc.dma_start(out=dr_h[p, :, :],
                                       in_=r_sb[:, :]), 16)


def emit_chunk_2A(nc, blk, S, cm, c, L, snapG):
    rtw, ktw, srw, vw2 = L["rtw"], L["ktw"], L["srw"], L["vw2"]
    gtt_su, sublk, rblk = L["gtt_su"], L["sublk"], L["rblk"]
    rL_sb, kL_sb, at_sb = L["rL_sb"], L["kL_sb"], L["at_sb"]
    ps_att, ps_o = L["ps_att"], L["ps_o"]
    o_sbuf = L["o_sbuf"]
    s0, coef, u_all = L["s0"], L["coef"], L["u_all"]
    em0s, em1s, fLrs = L["em0s"], L["em1s"], L["fLrs"]
    i64, bm, pm0, pm1 = L["i64"], L["bm"], L["pm0"], L["pm1"]
    rt_s, kt_s, srec_s, vtok_s = L["rt_s"], L["kt_s"], L["srec_s"], L["vtok_s"]
    ALUm, ALUa = ALU.mult, ALU.add
    w = c // 8

    if c % 8 == 0:
        @blk.sync
        def _(q, w=w):
            if c == 0:
                S.wait_snap(q, snapG)
            if w >= 2:
                S.w(q, "TE", cm.get(("2aom", (w - 2) * 8 + 7, NG - 1)))
            sl = "Daw%d" % (w % 2)
            for g in range(NG):
                S.inc(sl, q.dma_start(
                    out=rtw[w % 2][:, g * 512:(g + 1) * 512],
                    in_=rt_s[g, :, w * 512:(w + 1) * 512]), 16)
                S.inc(sl, q.dma_start(
                    out=ktw[w % 2][:, g * 512:(g + 1) * 512],
                    in_=kt_s[g, :, w * 512:(w + 1) * 512]), 16)
                S.inc(sl, q.dma_start(
                    out=srw[w % 2][:, g * 512:(g + 1) * 512],
                    in_=srec_s[g, :, w * 512:(w + 1) * 512]), 16)
            cm[("2awl", w)] = (sl, S.n(sl))

    @blk.sync
    def _(q, c=c):
        if c >= 2:
            S.w(q, "TE", cm.get(("2aom", c - 2, NG - 1)))
        cm[("2avl", c)] = S.dinc("Dav%d" % (c % 2), q.dma_start(
            out=vw2[c % 2][:, :], in_=vtok_s[c, :, :]))

    for g in range(NG):
        i2 = (c * NG + g) % 2
        io = (c * NG + g) % 2
        cc = c * NG + g
        co = g * 512 + (c % 8) * 64

        @blk.vector
        def _(v, c=c, g=g, i2=i2, w=w, cc=cc, co=co):
            if c == 0 and g == 0:
                S.wait_snap(v, snapG)
                S.wt(v, cm["s0ld"])
            S.wt(v, cm[("2awl", w)])
            if not (c == 0 and g == 0):
                pk = (c, g - 1) if g > 0 else (c - 1, NG - 1)
                S.w(v, "TE", cm.get(("2aom", pk[0], pk[1])))
            v.scalar_tensor_tensor(
                gtt_su[:, :], s0[:, g * 64:(g + 1) * 64],
                coef[:, cc:cc + 1],
                srw[w % 2][:, g * 512 + (c % 8) * 64:
                           g * 512 + (c % 8) * 64 + 64],
                ALUm, ALUa)
            v.tensor_scalar_mul(rL_sb[:, :], rtw[w % 2][:, co + 32:co + 64],
                                fLrs[:, cc:cc + 1])
            S.fence(v, "VE", v.tensor_scalar_mul(
                kL_sb[:, :], ktw[w % 2][:, co:co + 32],
                em0s[:, cc:cc + 1]))
            gbc = gtt_su[:, :].rearrange("p (a v) -> p a v", a=1).broadcast_to(
                [128, 2, 64])
            for pp in range(2):
                pmv = (pm0 if pp == 0 else pm1)[:, :].rearrange(
                    "p (a v) -> p a v", v=64)
                v.scalar_tensor_tensor(
                    sublk[2 * pp][:, :].rearrange("p (a v) -> p a v", v=64),
                    gbc, em0s[:, cc:cc + 1], pmv, ALUm, ALUm)
                v.scalar_tensor_tensor(
                    sublk[2 * pp + 1][:, :].rearrange("p (a v) -> p a v",
                                                      v=64),
                    gbc, em1s[:, cc:cc + 1], pmv, ALUm, ALUm)
            bmv = bm[:, :].rearrange("p (a t) -> p a t", t=32)
            for i, srcv in enumerate([rtw[w % 2][:, co:co + 32],
                                      rtw[w % 2][:, co + 32:co + 64],
                                      rL_sb[:, :]]):
                bc = srcv.rearrange("p (a t) -> p a t", a=1).broadcast_to(
                    [128, 4, 32])
                ii = v.tensor_mul(
                    rblk[i][:, :].rearrange("p (a t) -> p a t", t=32),
                    bc, bmv)
            cm[("2asu", c, g)] = S.inc("VE", ii, 1)

        @blk.tensor
        def _(te, c=c, g=g, i2=i2, w=w, co=co):
            if c == 0 and g == 0:
                S.wait_snap(te, snapG)
            S.wt(te, cm[("2awl", w)])
            S.w(te, "VE", cm[("2asu", c, g)])
            if c * NG + g >= 2:
                pi = c * NG + g - 2
                S.w(te, "VE", cm.get(("2aae", pi // NG, pi % NG)))
            else:
                S.w(te, "VE", cm["2amz"])
            te.matmul(ps_att[i2][0:32, 0:128], ktw[w % 2][:, co:co + 32],
                      rblk[0][:, :], start=True, stop=True)
            te.matmul(ps_att[i2][32:64, 128:256],
                      ktw[w % 2][:, co + 32:co + 64],
                      rblk[1][:, :], start=True, stop=True,
                      tile_position=(0, 32))
            ii = te.matmul(ps_att[i2][0:32, 256:384], kL_sb[:, :],
                           rblk[2][:, :], start=True, stop=True)
            cm[("2aam", c, g)] = S.inc("TE", ii, 1)

        @blk.vector
        def _(v, c=c, g=g, i2=i2):
            S.w(v, "TE", cm[("2aam", c, g)])
            if c * NG + g >= 2:
                pi = c * NG + g - 2
                S.w(v, "TE", cm.get(("2aom", pi // NG, pi % NG)))
            else:
                S.w(v, "VE", cm["2amz"])
            v.tensor_copy(
                at_sb[i2][0:32, :].rearrange("p (h t) -> p h t", t=64)[
                    :, :, 0:32],
                ps_att[i2][0:32, 0:128].rearrange("p (h t) -> p h t", t=32))
            v.tensor_copy(
                at_sb[i2][32:64, :].rearrange("p (h t) -> p h t", t=64)[
                    :, :, 32:64],
                ps_att[i2][32:64, 128:256].rearrange("p (h t) -> p h t",
                                                     t=32))
            ii = v.tensor_copy(
                at_sb[i2][0:32, :].rearrange("p (h t) -> p h t", t=64)[
                    :, :, 32:64],
                ps_att[i2][0:32, 256:384].rearrange("p (h t) -> p h t",
                                                    t=32))
            cm[("2aae", c, g)] = S.inc("VE", ii, 1)

        @blk.gpsimd
        def _(gp, c=c, g=g, i2=i2):
            S.w(gp, "VE", cm[("2aae", c, g)])
            ii = gp.affine_select(
                out=at_sb[i2][:, :].rearrange("p (b t) -> p b t", t=64),
                in_=at_sb[i2][:, :].rearrange("p (b t) -> p b t", t=64),
                compare_op=ALU.is_gt, fill=0.0, base=0,
                pattern=[[0, 4], [1, 64]], channel_multiplier=-1)
            cm[("2ams", c, g)] = S.inc("GE", ii, 1)

        @blk.vector
        def _(v, c=c, g=g, i2=i2):
            S.w(v, "GE", cm[("2ams", c, g)])
            last = None
            for h in range(4):
                c4 = (g * NCH + c) * 4
                last = v.scalar_tensor_tensor(
                    at_sb[i2][:, h * 64:(h + 1) * 64], i64[:, :],
                    u_all[:, c4 + h:c4 + h + 1],
                    at_sb[i2][:, h * 64:(h + 1) * 64],
                    ALUm, ALUa)
            cm[("2aad", c, g)] = S.inc("VE", last, 1)

        @blk.tensor
        def _(te, c=c, g=g, i2=i2, io=io, co=co, w=w):
            S.w(te, "VE", cm[("2aad", c, g)])
            S.wt(te, cm[("2avl", c)])
            if c * NG + g >= 2:
                pi = c * NG + g - 2
                S.w(te, "VE", cm.get(("2aoe", pi // NG, pi % NG)))
            last = None
            for pp in range(2):
                te.matmul(ps_o[2 * io + pp][:, 0:32], sublk[2 * pp][:, :],
                          rtw[w % 2][:, co:co + 32],
                          start=True, stop=False, skip_group_check=True)
                te.matmul(ps_o[2 * io + pp][:, 32:64],
                          sublk[2 * pp + 1][:, :],
                          rtw[w % 2][:, co + 32:co + 64],
                          start=False, stop=False, skip_group_check=True)
                for h2 in range(2):
                    h = pp * 2 + h2
                    last = te.matmul(
                        ps_o[2 * io + pp][64 * h2:64 * h2 + 64, 0:64],
                        vw2[c % 2][:, (g * 4 + h) * 64:(g * 4 + h) * 64 + 64],
                        at_sb[i2][:, h * 64:(h + 1) * 64],
                        start=False, stop=True, skip_group_check=True,
                        tile_position=(0, 64 * h2))
            cm[("2aom", c, g)] = S.inc("TE", last, 1)

        @blk.vector
        def _(v, c=c, g=g, io=io):
            S.w(v, "TE", cm[("2aom", c, g)])
            dvA, dvB = 2 * g, 2 * g + 1
            v.tensor_copy(
                o_sbuf[:, dvA * NT + c * 64:dvA * NT + c * 64 + 64],
                ps_o[2 * io][:, :])
            ii = v.tensor_copy(
                o_sbuf[:, dvB * NT + c * 64:dvB * NT + c * 64 + 64],
                ps_o[2 * io + 1][:, :])
            cm[("2aoe", c, g)] = S.inc("VE", ii, 1)


def emit_block_2B(nc, blk, S, cm, b, L, b2, snap2A):
    o_sbuf = L["o_sbuf"]
    indf, bcp, indb = L["indf"], L["bcp"], L["indb"]
    xlgw, wgw, wow = L["xlgw"], L["wgw"], L["wow"]
    sgsb, ogsb = L["sgsb"], L["ogsb"]
    sqsb, stt2, rst2, gtt = L["sqsb"], L["stt2"], L["rst2"], L["gtt"]
    xrd, frd, ost = L["xrd"], L["frd"], L["ost"]
    gnwT, gnbT = L["gnwT"], L["gnbT"]
    eps2 = L["eps_sb"]
    xlg_s, wgb_s, wob_s, ffn_s = L["xlg_s"], L["wgb_s"], L["wob_s"], L["ffn_s"]
    x_h, out_h = L["x_h"], L["out_h"]

    @blk.scalar
    def _(q, b=b):
        if b == 0:
            S.wait_snap(q, snap2A)
        if b >= 2:
            S.w(q, "TE", cm.get(("2bsgd", b - 2)))
        for d in range(DP):
            S.inc("Dbx%d" % (b % 2), q.dma_start(
                out=xlgw[b % 2][:, d * 512:(d + 1) * 512],
                in_=xlg_s[d, :, b * BL:(b + 1) * BL]), 16)
        cm[("2bxl", b)] = ("Dbx%d" % (b % 2), S.n("Dbx%d" % (b % 2)))

    with ExitStack() as ses:
        ps_sg = [ses.enter_context(nc.psum_tensor(
            "ps_sg%d_%d" % (b, i), [128, 512], F32)) for i in range(8)]
        for dvh in range(2):
            for d in range(DP):
                i = dvh * DP + d

                @blk.sync
                def _(q, b=b, dvh=dvh, d=d, i=i):
                    if (b, i) != (0, 0):
                        pk = (b, i - 2) if i >= 2 else (b - 1, 2 * DP - 2 + i)
                        S.w(q, "TE", cm.get(("2bgm", pk[0], pk[1])))
                    cm[("2bgl", b, i)] = S.dinc(
                        "Dbw%d" % (i % 2), q.dma_start(
                            out=wgw[i % 2][:, :],
                            in_=wgb_s[d * 128:(d + 1) * 128,
                                      dvh * 1024:(dvh + 1) * 1024]))

                @blk.tensor
                def _(te, b=b, dvh=dvh, d=d, i=i):
                    S.wt(te, cm[("2bgl", b, i)])
                    S.wt(te, cm[("2bxl", b)])
                    if d == 0 and dvh == 0:
                        if b > 0:
                            S.w(te, "SE", cm.get(("2bsil", b - 1, 1)))
                            S.w(te, "VE", cm.get(("2boc", b - 1, 1, 7)))
                    if d == 0 and dvh == 1:
                        S.w(te, "SE", cm.get(("2bsil", b, 0)))
                    ii = None
                    for dv8 in range(8):
                        ii = te.matmul(
                            ps_sg[dv8][:, :],
                            wgw[i % 2][:, dv8 * 128:(dv8 + 1) * 128],
                            xlgw[b % 2][:, d * 512:(d + 1) * 512],
                            start=(d == 0), stop=(d == DP - 1))
                    cm[("2bgm", b, i)] = S.inc("TE", ii, 1)

            @blk.scalar
            def _(sc, b=b, dvh=dvh):
                S.w(sc, "TE", cm[("2bgm", b, dvh * DP + DP - 1)])
                last = None
                for dv8 in range(8):
                    dv = dvh * 8 + dv8
                    last = sc.activation(sgsb[:, dv * 512:(dv + 1) * 512],
                                         ps_sg[dv8][:, :], AF.Silu)
                cm[("2bsil", b, dvh)] = S.inc("SE", last, 1)
        cm[("2bsgd", b)] = cm[("2bgm", b, 2 * DP - 1)]

    with ExitStack() as ges:
        ps_s2 = ges.enter_context(nc.psum_tensor("ps_s2_%d" % b, [32, 512], F32))
        ps_q2 = ges.enter_context(nc.psum_tensor("ps_q2_%d" % b, [32, 512], F32))
        for dv in range(DP):
            @blk.scalar
            def _(sc, b=b, dv=dv):
                if dv == 0:
                    S.wt(sc, cm["c2bld"])
                if dv >= 2:
                    S.w(sc, "TE", cm.get(("2bqm", b, dv - 2)))
                elif b > 0:
                    S.w(sc, "TE", cm.get(("2bqm", b - 1, DP - 2 + dv)))
                ii = sc.activation(
                    sqsb[dv % 2][:, :],
                    o_sbuf[:, dv * NT + b * BL:dv * NT + (b + 1) * BL],
                    AF.Square)
                cm[("2bsq", b, dv)] = S.inc("SE", ii, 1)

            @blk.tensor
            def _(te, b=b, dv=dv):
                if dv == 0:
                    S.wt(te, cm["c2bld"])
                    S.w(te, "SE", cm[("2bsil", b, 1)])
                    if b > 0:
                        S.w(te, "VE", cm.get(("2bst", b - 1)))
                te.matmul(ps_s2[:, :], indb[:, dv * 32:(dv + 1) * 32],
                          o_sbuf[:, dv * NT + b * BL:dv * NT + (b + 1) * BL],
                          start=(dv == 0), stop=(dv == DP - 1))
                S.w(te, "SE", cm[("2bsq", b, dv)])
                ii = te.matmul(ps_q2[:, :], indf[:, dv * 32:(dv + 1) * 32],
                               sqsb[dv % 2][:, :],
                               start=(dv == 0), stop=(dv == DP - 1))
                cm[("2bqm", b, dv)] = S.inc("TE", ii, 1)

        @blk.vector
        def _(v, b=b):
            S.w(v, "TE", cm[("2bqm", b, DP - 1)])
            v.tensor_scalar_mul(stt2[:, 0:512], ps_s2[:, :], -1.0 / 64.0)
            S.fence(v, "VE", v.tensor_scalar_mul(
                stt2[:, 512:1024], ps_q2[:, :], 1.0 / 64.0))
            S.fence(v, "VE", v.tensor_mul(
                stt2[:, 1024:1536], stt2[:, 0:512], stt2[:, 0:512]))
            ii = v.tensor_sub(stt2[:, 1536:2048], stt2[:, 512:1024],
                              stt2[:, 1024:1536])
            cm[("2bst", b)] = S.inc("VE", ii, 1)

        @blk.scalar
        def _(sc, b=b):
            S.w(sc, "VE", cm[("2bst", b)])
            sc.activation(rst2[:, :], stt2[:, 1536:2048], AF.Ln,
                          bias=eps2[0:32, :])
            cm[("2brs", b)] = S.inc("SE", sc.activation(
                rst2[:, :], rst2[:, :], AF.Exp, scale=-0.5), 1)

    with ExitStack() as oes:
        ps_nm2 = oes.enter_context(
            nc.psum_tensor("ps_nm2_%d" % b, [128, 512], F32))
        ps_rs2 = oes.enter_context(
            nc.psum_tensor("ps_rs2_%d" % b, [128, 512], F32))
        for dv in range(DP):
            @blk.tensor
            def _(te, b=b, dv=dv):
                S.w(te, "SE", cm[("2brs", b)])
                if dv > 0:
                    S.w(te, "VE", cm.get(("2bgt", b, dv - 1)))
                elif b > 0:
                    S.w(te, "VE", cm.get(("2bgt", b - 1, DP - 1)))
                te.matmul(ps_nm2[:, :], bcp[:, dv * 128:(dv + 1) * 128],
                          stt2[:, 0:512], start=True, stop=True)
                ii = te.matmul(ps_rs2[:, :], bcp[:, dv * 128:(dv + 1) * 128],
                               rst2[:, :], start=True, stop=True)
                cm[("2bbc", b, dv)] = S.inc("TE", ii, 1)

            @blk.vector
            def _(v, b=b, dv=dv):
                S.w(v, "TE", cm[("2bbc", b, dv)])
                if dv > 0:
                    S.w(v, "GE", cm.get(("2bog", b, dv - 1)))
                elif b > 0:
                    S.w(v, "GE", cm.get(("2bog", b - 1, DP - 1)))
                v.tensor_add(
                    gtt[0][:, :],
                    o_sbuf[:, dv * NT + b * BL:dv * NT + (b + 1) * BL],
                    ps_nm2[:, :])
                ii = v.tensor_mul(gtt[1][:, :], gtt[0][:, :], ps_rs2[:, :])
                cm[("2bgt", b, dv)] = S.inc("VE", ii, 1)

            @blk.vector
            def _(v, b=b, dv=dv):
                S.w(v, "VE", cm[("2bgt", b, dv)])
                ii = v.tensor_scalar(gtt[0][:, :], gtt[1][:, :], gnwT[:, 0:1],
                                     gnbT[:, 0:1], ALU.mult, ALU.add)
                cm[("2bt3", b, dv)] = S.inc("VE", ii, 1)

            @blk.gpsimd
            def _(gp, b=b, dv=dv):
                S.w(gp, "VE", cm[("2bt3", b, dv)])
                S.w(gp, "SE", cm[("2bsil", b, dv // 8)])
                ii = gp.tensor_mul(ogsb[:, dv * 512:(dv + 1) * 512],
                                   gtt[0][:, :],
                                   sgsb[:, dv * 512:(dv + 1) * 512])
                cm[("2bog", b, dv)] = S.inc("GE", ii, 1)

    with ExitStack() as wes:
        ps_wo = [wes.enter_context(nc.psum_tensor(
            "ps_wo%d_%d" % (b, i), [128, 512], F32)) for i in range(8)]
        for dh in range(2):
            for dv in range(DP):
                i = dh * DP + dv

                @blk.scalar
                def _(q, b=b, dh=dh, dv=dv, i=i):
                    if (b, i) != (0, 0):
                        pk = (b, i - 2) if i >= 2 else (b - 1, 2 * DP - 2 + i)
                        S.w(q, "TE", cm.get(("2bwm", pk[0], pk[1])))
                    cm[("2bwl", b, i)] = S.dinc(
                        "Dbo%d" % (i % 2), q.dma_start(
                            out=wow[i % 2][:, :],
                            in_=wob_s[dv * 128:(dv + 1) * 128,
                                      dh * 1024:(dh + 1) * 1024]))

                @blk.tensor
                def _(te, b=b, dh=dh, dv=dv, i=i):
                    S.wt(te, cm[("2bwl", b, i)])
                    S.w(te, "GE", cm[("2bog", b, dv)])
                    if dv == 0:
                        if dh == 0:
                            S.w(te, "VE", cm[("2bgt", b, DP - 1)])
                        if dh == 0 and b > 0:
                            S.w(te, "VE", cm.get(("2boc", b - 1, 1, 7)))
                        if dh == 1:
                            S.w(te, "VE", cm.get(("2boc", b, 0, 7)))
                    ii = None
                    for d8 in range(8):
                        ii = te.matmul(
                            ps_wo[d8][:, :],
                            wow[i % 2][:, d8 * 128:(d8 + 1) * 128],
                            ogsb[:, dv * 512:(dv + 1) * 512],
                            start=(dv == 0), stop=(dv == DP - 1))
                    cm[("2bwm", b, i)] = S.inc("TE", ii, 1)

            for d8 in range(8):
                dglob = dh * 8 + d8

                @blk.sync
                def _(q, b=b, dh=dh, d8=d8, dglob=dglob):
                    j = (b * 2 + dh) * 8 + d8
                    if j >= 2:
                        S.w(q, "VE", cm.get(("2bocj", j - 2)))
                    S.inc("Dxf%d" % (j % 2), q.dma_start(
                        out=xrd[j % 2][:, :],
                        in_=x_h[dglob * 128:(dglob + 1) * 128,
                                b * BL:(b + 1) * BL]), 16)
                    cm[("2bxf", j)] = S.dinc(
                        "Dxf%d" % (j % 2), q.dma_start(
                            out=frd[j % 2][:, :],
                            in_=ffn_s[dglob * 128:(dglob + 1) * 128,
                                      b * BL:(b + 1) * BL]))

                @blk.vector
                def _(v, b=b, dh=dh, d8=d8, dglob=dglob):
                    j = (b * 2 + dh) * 8 + d8
                    S.wt(v, cm[("2bxf", j)])
                    S.w(v, "TE", cm[("2bwm", b, dh * DP + DP - 1)])
                    if j >= 2:
                        S.wt(v, cm.get(("2bos", j - 2)))
                    v.tensor_add(ost[j % 2][:, :], xrd[j % 2][:, :],
                                 frd[j % 2][:, :])
                    ii = v.tensor_add(ost[j % 2][:, :], ost[j % 2][:, :],
                                      ps_wo[d8][:, :])
                    cm[("2boc", b, dh, d8)] = S.inc("VE", ii, 1)
                    cm[("2bocj", j)] = cm[("2boc", b, dh, d8)]

                @blk.scalar
                def _(sc, b=b, dh=dh, d8=d8, dglob=dglob):
                    j = (b * 2 + dh) * 8 + d8
                    S.w(sc, "VE", cm[("2boc", b, dh, d8)])
                    cm[("2bos", j)] = S.dinc(
                        "Dou%d" % (j % 2), sc.dma_start(
                            out=out_h[dglob * 128:(dglob + 1) * 128,
                                      b * BL:(b + 1) * BL],
                            in_=ost[j % 2][:, :]))


_cache = {}


def _consts():
    if "c" in _cache:
        return _cache["c"]
    i64 = np.eye(64).astype(BF)
    id128 = np.eye(128).astype(BF)
    bmc = np.zeros((128, 128), np.float32)
    for h in range(4):
        bmc[32 * h:32 * h + 32, 32 * h:32 * h + 32] = 1.0
    pm0c = np.zeros((128, 128), np.float32)
    pm1c = np.zeros((128, 128), np.float32)
    for h2 in range(2):
        pm0c[32 * h2:32 * h2 + 32, 64 * h2:64 * h2 + 64] = 1.0
        pm1c[64 + 32 * h2:64 + 32 * h2 + 32, 64 * h2:64 * h2 + 64] = 1.0
    seg4 = np.zeros((128, 4), np.float32)
    for h in range(4):
        seg4[32 * h:32 * (h + 1), h] = 1.0
    indb = np.zeros((128, 512), np.float32)
    for i in range(16):
        indb[0:64, i * 32 + 2 * i] = 1.0
        indb[64:128, i * 32 + 2 * i + 1] = 1.0
    bcp = np.zeros((32, 2048), np.float32)
    for i in range(16):
        bcp[2 * i, i * 128:i * 128 + 64] = 1.0
        bcp[2 * i + 1, i * 128 + 64:i * 128 + 128] = 1.0
    c = dict(i64_in=i64, id128_in=id128, bm_in=bmc.astype(BF),
             pm0_in=pm0c.astype(BF), pm1_in=pm1c.astype(BF),
             seg4_in=seg4.astype(BF), indb_in=indb.astype(BF),
             indf_in=indb.astype(np.float32), bcp_in=bcp,
             ones1_in=np.ones((128, 1), np.float32),
             onesk1_in=np.ones((1, 128), np.float32),
             ones64_in=np.ones((128, 64), np.float32))
    _cache["c"] = c
    return c


def kernel(**inp):
    f32 = np.float32
    x = np.asarray(inp["L0_attn_hidden_states"], f32)
    con = _consts()

    shared = {
        "wx1": np.ascontiguousarray(np.asarray(inp["L0_attn_Wx1"], f32)),
        "wx2r": np.ascontiguousarray(
            np.asarray(inp["L0_attn_Wx2"], f32).transpose(1, 0, 2)
            .reshape(32, 5 * 2048)),
        "wr": np.ascontiguousarray(np.asarray(inp["L0_attn_Wr"], f32)),
        "ww1": np.ascontiguousarray(np.asarray(inp["L0_attn_Ww1"], f32)),
        "ww2": np.ascontiguousarray(np.asarray(inp["L0_attn_Ww2"], f32)),
        "wk": np.ascontiguousarray(np.asarray(inp["L0_attn_Wk"], f32)),
        "wv": np.ascontiguousarray(np.asarray(inp["L0_attn_Wv"], f32)),
        "wg": np.ascontiguousarray(np.asarray(inp["L0_attn_Wg"], f32)),
        "wo": np.ascontiguousarray(np.asarray(inp["L0_attn_Wo"], f32)),
        "bwT_in": np.ascontiguousarray(
            np.asarray(inp["L0_attn_bw"], f32).reshape(8, 128).T),
        "muT_in": np.ascontiguousarray(
            np.asarray(inp["L0_attn_mu_x"], f32).reshape(16, 128).T),
        "bonusT_in": np.ascontiguousarray(
            np.asarray(inp["L0_attn_bonus"], f32).reshape(1024)
            .reshape(8, 128).T),
        "gnwT_in": np.ascontiguousarray(
            np.tile(np.asarray(inp["L0_attn_gn_w"], f32), 2)[:, None]),
        "gnbT_in": np.ascontiguousarray(
            np.tile(np.asarray(inp["L0_attn_gn_b"], f32), 2)[:, None]),
        "sgT_in": np.ascontiguousarray(
            np.asarray(inp["L0_ffn_s_gate"], f32).reshape(64, 128).T),
        "suT_in": np.ascontiguousarray(
            np.asarray(inp["L0_ffn_s_up"], f32).reshape(64, 128).T),
        "sdT_in": np.ascontiguousarray(
            np.asarray(inp["L0_ffn_s_down"], f32).reshape(16, 128).T),
        "wgate8": np.ascontiguousarray(
            np.asarray(inp["L0_ffn_w_gate"]).reshape(16, 128, 64, 128)
            .transpose(2, 1, 0, 3).reshape(64, 128, 16 * 128)),
        "wup8": np.ascontiguousarray(
            np.asarray(inp["L0_ffn_w_up"]).reshape(16, 128, 64, 128)
            .transpose(2, 1, 0, 3).reshape(64, 128, 16 * 128)),
        "wdn8": np.ascontiguousarray(
            np.asarray(inp["L0_ffn_w_down"]).reshape(64, 128, 16, 128)
            .transpose(2, 1, 0, 3).reshape(16, 128, 64 * 128)),
    }
    shared.update(con)

    in_maps = []
    for core in range(8):
        b, h = core // 2, core % 2
        xfm = np.ascontiguousarray(x[b, h * NT:(h + 1) * NT, :].T)
        if h == 0:
            xprev = np.zeros((128, DP), f32)
            gate = np.zeros((128, 1), f32)
        else:
            xprev = np.ascontiguousarray(
                x[b, NT - 1, :].reshape(16, 128).T)
            gate = np.ones((128, 1), f32)
        m = dict(shared)
        m["x"] = xfm
        m["xprev"] = xprev
        m["gate"] = gate
        in_maps.append(m)

    if "nc" not in _cache:
        _cache["nc"] = build_program()
    nc = _cache["nc"]
    res = run_bass_kernel_spmd(nc, in_maps, core_ids=list(range(8)))
    out = np.zeros((4, 4096, 2048), f32)
    for core in range(8):
        b, h = core // 2, core % 2
        out[b, h * NT:(h + 1) * NT, :] = np.asarray(
            res.results[core]["out"], f32).T
    if DEBUG:
        kernel._last = res
    return out
